# version 20
# baseline (speedup 1.0000x reference)
import numpy as np
from contextlib import ExitStack

import concourse.bass as bass
import concourse.mybir as mybir

F32 = mybir.dt.float32
BF16 = mybir.dt.bfloat16
AF = mybir.ActivationFunctionType
ALU = mybir.AluOpType
AX = mybir.AxisListType

TOK = 2048
D = 1024
SEQ = 8192
NEG = -1e30


_NAME_N = [0]


def _un(name):
    _NAME_N[0] += 1
    return f"{name}_{_NAME_N[0]}"


class Rec:
    ENGS = ("sync", "tensor", "vector", "scalar", "gpsimd")

    def __init__(self, nc, ctx):
        self.nc = nc
        self.ops = {e: [] for e in self.ENGS}
        self.sem = {e: ctx.enter_context(nc.semaphore(f"s_{e}")) for e in self.ENGS}
        self.cnt = {e: 0 for e in self.ENGS}
        self.dsem = {q: ctx.enter_context(nc.semaphore(f"d_{q}")) for q in ("sync", "gpsimd")}
        self.dcnt = {q: 0 for q in ("sync", "gpsimd")}
        self.csem = ctx.enter_context(nc.semaphore("s_cc"))
        self.ccnt = 0

    def cc(self, make):
        self.ccnt += 1
        n = self.ccnt
        sem = self.csem
        self.ops["gpsimd"].append(lambda e, make=make, sem=sem: make(e).then_inc(sem, 1))
        return n

    def cwait(self, eng, val):
        if val is None or val <= 0:
            return
        sem = self.csem
        self.ops[eng].append(lambda e, sem=sem, val=val: e.wait_ge(sem, val))

    SELF_SYNC = ("vector", "scalar", "gpsimd")

    def op(self, eng, make):
        sem = self.sem[eng]
        if eng in self.SELF_SYNC and self.cnt[eng] > 0:
            prev = self.cnt[eng]
            self.ops[eng].append(lambda e, sem=sem, prev=prev: e.wait_ge(sem, prev))
        self.cnt[eng] += 1
        n = self.cnt[eng]
        self.ops[eng].append(lambda e, make=make, sem=sem: make(e).then_inc(sem, 1))
        return n

    def dma(self, q, out, in_, **kw):
        self.dcnt[q] += 16
        n = self.dcnt[q]
        sem = self.dsem[q]
        self.ops[q].append(
            lambda e, out=out, in_=in_, kw=kw, sem=sem: e.dma_start(out=out, in_=in_, **kw).then_inc(sem, 16))
        return n

    def wait(self, eng, prod, val):
        if val is None or val <= 0:
            return
        sem = self.sem[prod]
        self.ops[eng].append(lambda e, sem=sem, val=val: e.wait_ge(sem, val))

    def dwait(self, eng, q, val):
        if val is None or val <= 0:
            return
        sem = self.dsem[q]
        self.ops[eng].append(lambda e, sem=sem, val=val: e.wait_ge(sem, val))

    def finalize(self, blk):
        for eng in self.ENGS:
            fns = self.ops[eng]
            getattr(blk, eng)(lambda e, fns=fns: [f(e) for f in fns])


def build_ident(r, sb_):
    io = sb_("io_", [128, 128], F32)
    pi = sb_("pi_", [128, 1], F32)
    ident = sb_("ident_", [128, 128], BF16)
    ones_bf = sb_("onesb_", [128, 1], BF16)
    ones_f = sb_("onesf_", [128, 128], F32)
    r.op("gpsimd", lambda e: e.iota(io[:], pattern=[[1, 128]], base=0, channel_multiplier=0,
                                    allow_small_or_imprecise_dtypes=True))
    g2 = r.op("gpsimd", lambda e: e.iota(pi[:], pattern=[[0, 1]], base=0, channel_multiplier=1,
                                         allow_small_or_imprecise_dtypes=True))
    r.wait("vector", "gpsimd", g2)
    r.op("vector", lambda e: e.tensor_tensor(out=ident[:], in0=io[:], in1=pi[:].broadcast_to([128, 128]),
                                             op=ALU.is_equal))
    r.op("vector", lambda e: e.memset(ones_bf[:], 1.0))
    vc = r.op("vector", lambda e: e.memset(ones_f[0:1, :], 1.0))
    return ident, ones_bf, ones_f, vc


def emit_ln(r, x_ap, out_ap, scr, D_=1024, eps=1e-5, swait=None):
    sq, st = scr["sq"], scr["st"]
    s_sum, s_sq, s_m, s_var, s_rstd, s_tmp = (st[:, i:i + 1] for i in range(6))
    r.wait("scalar", "vector", r.cnt["vector"] if swait is None else swait)
    sc = r.op("scalar", lambda e: e.activation(out=sq[:, 0:D_], in_=x_ap, func=AF.Square))
    r.wait("vector", "scalar", sc)
    r.op("vector", lambda e: e.reduce_sum(out=s_sum, in_=x_ap, axis=AX.X))
    r.op("vector", lambda e: e.reduce_sum(out=s_sq, in_=sq[:, 0:D_], axis=AX.X))
    r.op("vector", lambda e: e.tensor_scalar(out=s_m, in0=s_sum, scalar1=1.0 / D_, scalar2=None, op0=ALU.mult))
    r.op("vector", lambda e: e.tensor_scalar(out=s_tmp, in0=s_sq, scalar1=1.0 / D_, scalar2=None, op0=ALU.mult))
    r.op("vector", lambda e: e.tensor_tensor(out=s_var, in0=s_m, in1=s_m, op=ALU.mult))
    r.op("vector", lambda e: e.tensor_tensor(out=s_var, in0=s_tmp, in1=s_var, op=ALU.subtract))
    vc = r.op("vector", lambda e: e.tensor_scalar(out=s_var, in0=s_var, scalar1=eps, scalar2=None, op0=ALU.add))
    r.wait("scalar", "vector", vc)
    sc2 = r.op("scalar", lambda e: e.activation(out=s_rstd, in_=s_var, func=AF.Sqrt))
    r.wait("vector", "scalar", sc2)
    r.op("vector", lambda e: e.reciprocal(out=s_rstd, in_=s_rstd))
    r.op("vector", lambda e: e.tensor_tensor(out=sq[:, 0:D_], in0=x_ap,
                                             in1=s_m.broadcast_to([128, D_]), op=ALU.subtract))
    vc2 = r.op("vector", lambda e: e.tensor_tensor(out=out_ap, in0=sq[:, 0:D_],
                                                   in1=s_rstd.broadcast_to([128, D_]), op=ALU.mult))
    return vc2


def emit_quant(r, x_ap, out_ap, scr, nelem, qb_ap=None):
    qb, qc, qbm, qsc, qr, sg = (qb_ap if qb_ap is not None else scr["qb"],
                                scr["qc"], scr["qbm"],
                                scr["qsc"], scr["qr"], scr["qsg"])
    nb = nelem // 16
    v_prev = r.cnt["vector"]
    r.wait("scalar", "vector", v_prev)
    sgc = r.op("scalar", lambda e: e.sign(out=sg[:, :nelem], in_=x_ap))
    abc = r.op("scalar", lambda e: e.activation(out=x_ap, in_=x_ap, func=AF.Abs))
    r.wait("vector", "scalar", abc)
    r.op("vector", lambda e: e.reduce_max(out=qbm[:, :nb], in_=x_ap.rearrange("p (a b) -> p a b", b=16),
                                          axis=AX.X, apply_absolute_value=True))
    r.op("vector", lambda e: e.tensor_scalar(out=qsc[:, :nb], in0=qbm[:, :nb], scalar1=1.0 / 6.0,
                                             scalar2=1e-12, op0=ALU.mult, op1=ALU.max))
    r.op("vector", lambda e: e.reciprocal(out=qr[:, :nb], in_=qsc[:, :nb]))
    vnorm = r.op("vector", lambda e: e.tensor_tensor(out=x_ap.rearrange("p (a b) -> p a b", b=16),
                                                     in0=x_ap.rearrange("p (a b) -> p a b", b=16),
                                                     in1=qr[:, :nb].broadcast_to([128, nb, 16]), op=ALU.mult))
    r.op("vector", lambda e: e.tensor_scalar(out=qb[:, :nelem], in0=x_ap, scalar1=0.25,
                                             scalar2=None, op0=ALU.is_gt))
    for t in (0.75, 1.25, 1.75):
        r.op("vector", lambda e, t=t: e.scalar_tensor_tensor(out=qb[:, :nelem], in0=x_ap,
                                                             scalar=t, in1=qb[:, :nelem],
                                                             op0=ALU.is_gt, op1=ALU.add))
    r.op("vector", lambda e: e.tensor_scalar(out=qc[:, :nelem], in0=x_ap, scalar1=2.5,
                                             scalar2=None, op0=ALU.is_gt))
    for t in (3.5, 5.0, 5.0):
        r.op("vector", lambda e, t=t: e.scalar_tensor_tensor(out=qc[:, :nelem], in0=x_ap,
                                                             scalar=t, in1=qc[:, :nelem],
                                                             op0=ALU.is_gt, op1=ALU.add))
    r.op("vector", lambda e: e.scalar_tensor_tensor(out=qb[:, :nelem], in0=qb[:, :nelem], scalar=0.5,
                                                    in1=qc[:, :nelem], op0=ALU.mult, op1=ALU.add))
    r.op("vector", lambda e: e.tensor_tensor(out=qb[:, :nelem].rearrange("p (a b) -> p a b", b=16),
                                             in0=qb[:, :nelem].rearrange("p (a b) -> p a b", b=16),
                                             in1=qsc[:, :nb].broadcast_to([128, nb, 16]), op=ALU.mult))
    vc = r.op("vector", lambda e: e.tensor_tensor(out=out_ap, in0=qb[:, :nelem], in1=sg[:, :nelem],
                                                  op=ALU.mult))
    return vc


class PsumPool:
    def __init__(self, tensors):
        self.slots = [[t, None, 0] for t in tensors]
        self.i = 0

    def acquire(self, r, prod_eng):
        slot = self.slots[self.i % len(self.slots)]
        self.i += 1
        t, ceng, ccnt = slot
        if ceng is not None:
            r.wait(prod_eng, ceng, ccnt)
        return t, slot

    @staticmethod
    def release(slot, eng, cnt):
        slot[1], slot[2] = eng, cnt


def transpose_tiles(r, ident, pairs, pp, eng_evac="scalar"):
    for src, dst in pairs:
        t, slot = pp.acquire(r, "tensor")
        tb = t[:].bitcast(BF16)
        pcols = src.shape[0]
        rows = src.shape[1]
        tc = r.op("tensor", lambda e, tb=tb, src=src, pcols=pcols: e.transpose(tb[:, 0:pcols], src, ident[:]))
        r.wait(eng_evac, "tensor", tc)
        if eng_evac == "scalar":
            ec = r.op("scalar", lambda e, tb=tb, dst=dst, pcols=pcols, rows=rows: e.copy(out=dst, in_=tb[0:rows, 0:pcols]))
        else:
            ec = r.op("vector", lambda e, tb=tb, dst=dst, pcols=pcols, rows=rows: e.tensor_copy(out=dst, in_=tb[0:rows, 0:pcols]))
        PsumPool.release(slot, eng_evac, ec)


def quant_scratch(sb_, width=2816):
    return dict(
        sq=sb_("sq", [128, 1024], F32),
        st=sb_("st", [128, 8], F32),
        qb=sb_("qb", [128, width], BF16),
        qc=sb_("qc", [128, width], BF16),
        qbm=sb_("qbm", [128, width // 16], F32),
        qsc=sb_("qsc", [128, width // 16], F32),
        qr=sb_("qr", [128, width // 16], F32),
        qsg=sb_("qsg", [128, width], BF16),
    )


def emit_l0_ffn(nc, r, ctx, x1_d, wsl_d, wdsl_d, wqin_d, wqout_d, o_d):
    I = 2816
    IC = I // 128
    NBLK = [512, 512, 512, 512, 512, 256]
    if True:
        sb_ = lambda name, shape, dt=F32: ctx.enter_context(nc.sbuf_tensor(_un(name), shape, dt))[:]
        ident, ones_bf, ones_f, _ = build_ident(r, sb_)
        wnat = sb_("wnat", [128, 2816], F32)
        wgT = sb_("wgT", [128, 8 * I], BF16)
        wuT = sb_("wuT", [128, 8 * I], BF16)
        wdT = sb_("wdT", [128, IC * 1024], BF16)
        xqf = sb_("xqf", [128, 8 * TOK], BF16)
        h_f = sb_("h_f", [128, 1024], F32)
        hqf_t = sb_("hqf_t", [128, 2 * 512], BF16)
        scr = quant_scratch(sb_)
        xt = wnat[:, 0:1024]
        g_fs = [scr["sq"][:, 0:512], scr["sq"][:, 512:1024]]
        h2_fs = [wnat[:, 1536:2048], wnat[:, 2048:2560]]
        p_g = ctx.enter_context(nc.psum_tensor(_un("p_g"), [128, 512], F32))
        p_u = ctx.enter_context(nc.psum_tensor(_un("p_u"), [128, 512], F32))
        p_g2 = ctx.enter_context(nc.psum_tensor(_un("p_g2"), [128, 512], F32))
        p_u2 = ctx.enter_context(nc.psum_tensor(_un("p_u2"), [128, 512], F32))
        p_d0 = ctx.enter_context(nc.psum_tensor(_un("p_d0"), [128, 512], F32))
        p_d1 = ctx.enter_context(nc.psum_tensor(_un("p_d1"), [128, 512], F32))
        p_t0 = ctx.enter_context(nc.psum_tensor(_un("p_t0"), [128, 512], F32))
        p_t1 = ctx.enter_context(nc.psum_tensor(_un("p_t1"), [128, 512], F32))
        pp = PsumPool([p_t0, p_t1])

        wstg = xqf[:, 0:8960]
        hist = []
        for idx in range(7):
            if idx < 6:
                w = idx % 2
                wsl = wnat[:, w * 1024:(w + 1) * 1024]
                qsl = scr["qb"][:, w * 1024:(w + 1) * 1024]
                width = 1024
                src = wsl_d[idx * 128:(idx + 1) * 128, :]
                dcol = idx * 1024
                gi = idx - 2
            else:
                wsl = wnat[:, 0:2816]
                qsl = scr["qb"][:, 0:2816]
                width = 2816
                src = wdsl_d
                dcol = 6144
                gi = idx - 1
            guard_v = hist[gi][0] if gi >= 0 else 0
            guard_t = hist[gi][1] if gi >= 0 else 0
            r.wait("sync", "vector", guard_v)
            dn = r.dma("sync", wsl, src)
            r.dwait("vector", "sync", dn)
            r.dwait("scalar", "sync", dn)
            r.wait("vector", "tensor", guard_t)
            vq = emit_quant(r, wsl, qsl[:, 0:width], scr, width, qb_ap=qsl)
            r.wait("tensor", "vector", vq)
            pairs = [(qsl[:, j * 128:(j + 1) * 128], wstg[:, dcol + j * 128: dcol + (j + 1) * 128])
                     for j in range(width // 128)]
            transpose_tiles(r, ident, pairs, pp)
            hist.append((vq, r.cnt["tensor"]))
        r.wait("sync", "vector", hist[-1][0])
        r.wait("sync", "scalar", r.cnt["scalar"])
        dwc = r.dma("sync", wqin_d, wstg[:])
        r.dwait("gpsimd", "sync", dwc)
        ncc = r.cc(lambda e: e.collective_compute(
            "AllGather", ALU.bypass, replica_groups=[list(range(8))],
            ins=[wqin_d.opt()], outs=[wqout_d.opt()]))
        r.cwait("gpsimd", ncc)
        for g in range(44):
            rk, s_ = divmod(g, 6)
            dstT, i_ = (wgT, g) if g < 22 else (wuT, g - 22)
            for kc in range(8):
                r.dma("gpsimd", dstT[:, kc * I + i_ * 128: kc * I + (i_ + 1) * 128],
                      wqout_d[rk * 128:(rk + 1) * 128, s_ * 1024 + kc * 128: s_ * 1024 + (kc + 1) * 128])
        for rk in range(8):
            for j in range(IC):
                r.dma("gpsimd", wdT[:, j * 1024 + rk * 128: j * 1024 + (rk + 1) * 128],
                      wqout_d[rk * 128:(rk + 1) * 128, 6144 + j * 128: 6144 + (j + 1) * 128])
        reasm = r.dcnt["gpsimd"]
        r.dwait("scalar", "sync", dwc)
        r.dwait("vector", "sync", dwc)

        for tt in range(16):
            t_prev = r.cnt["tensor"]
            dn = r.dma("sync", xt[:], x1_d[tt * 128:(tt + 1) * 128, :])
            r.dwait("vector", "sync", dn)
            r.dwait("scalar", "sync", dn)
            emit_ln(r, xt[:], h_f[:], scr)
            r.wait("vector", "tensor", t_prev)
            vq = emit_quant(r, h_f[:], scr["qb"][:, 0:1024], scr, 1024)
            r.wait("tensor", "vector", vq)
            pairs = [(scr["qb"][:, j * 128:(j + 1) * 128],
                      xqf[:, j * TOK + tt * 128: j * TOK + (tt + 1) * 128]) for j in range(8)]
            transpose_tiles(r, ident, pairs, pp)
            r.wait("sync", "scalar", r.cnt["scalar"])

        r.wait("tensor", "scalar", r.cnt["scalar"])
        r.dwait("tensor", "gpsimd", reasm)
        dcons = [None, None]
        prev_store = 0
        silu_cnt = [0, 0]
        h2_cnt = [0, 0]
        tc_cnt = [0, 0]
        down_cnt = [0, 0]
        pend_down = [None]
        p_gs, p_us = [p_g, p_g2], [p_u, p_u2]
        starts = [0, 512, 1024, 1536, 2048, 2560]
        blocks = [(tt, bi, starts[bi]) for tt in range(16) for bi in range(6)]
        gate_cnt = [0, 0]
        up_cnt = [0, 0]

        def emit_gateup(idx):
            tt, bi, n0 = blocks[idx]
            nb = NBLK[bi]
            w = idx % 2
            pg, pu = p_gs[w], p_us[w]
            r.wait("tensor", "scalar", silu_cnt[w])
            for kc in range(8):
                lh = xqf[:, kc * TOK + tt * 128: kc * TOK + (tt + 1) * 128]
                gate_cnt[w] = r.op("tensor", lambda e, lh=lh, kc=kc, n0=n0, nb=nb, pg=pg:
                                   e.matmul(pg[:, 0:nb], lh, wgT[:, kc * I + n0: kc * I + n0 + nb],
                                            start=(kc == 0), stop=(kc == 7)))
            r.wait("tensor", "vector", h2_cnt[w])
            for kc in range(8):
                lh = xqf[:, kc * TOK + tt * 128: kc * TOK + (tt + 1) * 128]
                up_cnt[w] = r.op("tensor", lambda e, lh=lh, kc=kc, n0=n0, nb=nb, pu=pu:
                                 e.matmul(pu[:, 0:nb], lh, wuT[:, kc * I + n0: kc * I + n0 + nb],
                                          start=(kc == 0), stop=(kc == 7)))

        def emit_silu(idx):
            w = idx % 2
            nb = NBLK[blocks[idx][1]]
            r.wait("scalar", "tensor", gate_cnt[w])
            r.wait("scalar", "vector", h2_cnt[w])
            sl = r.op("scalar", lambda e, nb=nb, pg=p_gs[w], g_f=g_fs[w]: e.activation(
                out=g_f[:, 0:nb], in_=pg[:, 0:nb], func=AF.Silu))
            silu_cnt[w] = sl

        emit_gateup(0)
        emit_silu(0)
        for idx in range(96):
            tt, bi, n0 = blocks[idx]
            nb = NBLK[bi]
            w = idx % 2
            pg, pu = p_gs[w], p_us[w]
            g_f, h2_f = g_fs[w], h2_fs[w]
            qsl = scr["qb"][:, w * 1024:w * 1024 + 512]
            umm = up_cnt[w]
            sl = silu_cnt[w]
            if idx + 1 < 96:
                emit_gateup(idx + 1)
            if pend_down[0] is not None:
                pend_down[0]()
                pend_down[0] = None
            r.wait("vector", "tensor", umm)
            r.wait("vector", "scalar", sl)
            h2 = r.op("vector", lambda e, nb=nb, pu=pu, g_f=g_f, h2_f=h2_f: e.tensor_tensor(
                out=h2_f[:, 0:nb], in0=pu[:, 0:nb], in1=g_f[:, 0:nb], op=ALU.mult))
            h2_cnt[w] = h2
            r.wait("vector", "tensor", tc_cnt[w])
            vq = emit_quant(r, h2_f[:, 0:nb], qsl[:, 0:nb], scr, nb, qb_ap=qsl)
            if idx + 1 < 96:
                emit_silu(idx + 1)
            r.wait("scalar", "tensor", down_cnt[w])
            r.wait("tensor", "vector", vq)
            pairs = [(qsl[:, j * 128:(j + 1) * 128], hqf_t[:, w * 512 + j * 128: w * 512 + (j + 1) * 128])
                     for j in range(nb // 128)]
            transpose_tiles(r, ident, pairs, pp)
            tc_cnt[w] = r.cnt["tensor"]
            tr_done = r.cnt["scalar"]

            def mk_down(bi=bi, nb=nb, n0=n0, w=w, tr_done=tr_done):
                def emit():
                    r.wait("tensor", "scalar", tr_done)
                    if bi == 0:
                        for cn in range(2):
                            r.wait("tensor", "vector", dcons[cn])
                    for cn, p_dn in enumerate((p_d0, p_d1)):
                        for j in range(nb // 128):
                            ic = (n0 // 128) + j
                            first = (bi == 0 and j == 0)
                            lastmm = (bi == len(NBLK) - 1 and j == (nb // 128) - 1)
                            r.op("tensor", lambda e, j=j, ic=ic, p_dn=p_dn, cn=cn, first=first,
                                 lastmm=lastmm, w=w:
                                 e.matmul(p_dn[:, :], hqf_t[:, w * 512 + j * 128: w * 512 + (j + 1) * 128],
                                          wdT[:, ic * 1024 + cn * 512: ic * 1024 + (cn + 1) * 512],
                                          start=first, stop=lastmm))
                    down_cnt[w] = r.cnt["tensor"]
                return emit
            pend_down[0] = mk_down()
            if bi != 5:
                continue
            pend_down[0]()
            pend_down[0] = None
            down_done = r.cnt["tensor"]
            dn = r.dma("sync", xt[:], x1_d[tt * 128:(tt + 1) * 128, :])
            r.dwait("vector", "sync", dn)
            r.dwait("vector", "sync", prev_store)
            r.wait("vector", "tensor", down_done)
            v0 = r.op("vector", lambda e: e.tensor_tensor(out=h_f[:, 0:512], in0=xt[:, 0:512],
                                                          in1=p_d0[:, :], op=ALU.add))
            v1 = r.op("vector", lambda e: e.tensor_tensor(out=h_f[:, 512:1024], in0=xt[:, 512:1024],
                                                          in1=p_d1[:, :], op=ALU.add))
            dcons = [v0, v1]
            r.wait("sync", "vector", v1)
            prev_store = r.dma("sync", o_d[tt * 128:(tt + 1) * 128, :], h_f[:])
        r.dwait("sync", "sync", r.dcnt["sync"])


def build_l0_ffn():
    nc = bass.Bass("TRN2", target_bir_lowering=False)
    x1_d = nc.dram_tensor("x1", [TOK, D], F32, kind="ExternalInput").ap()
    wg_d = nc.dram_tensor("wg", [2816, D], F32, kind="ExternalInput").ap()
    wu_d = nc.dram_tensor("wu", [2816, D], F32, kind="ExternalInput").ap()
    wd_d = nc.dram_tensor("wd", [D, 2816], F32, kind="ExternalInput").ap()
    o_d = nc.dram_tensor("o_x2", [TOK, D], F32, kind="ExternalOutput").ap()
    with ExitStack() as ctx:
        r = Rec(nc, ctx)
        with ExitStack() as s:
            emit_l0_ffn(nc, r, s, x1_d, wg_d, wu_d, wd_d, o_d)
        blk = ctx.enter_context(nc.Block())
        r.finalize(blk)
    return nc


CMW = 10112


def emit_l0_attn(nc, r, ctx, xb_d, xo_d, cm_d, wqT_d, wkvd_d, wkupT_d, wvupT_d, wo_d, o_d):
    SCL = 1.0 / np.sqrt(128.0)
    if True:
        sb_ = lambda name, shape, dt=F32: ctx.enter_context(nc.sbuf_tensor(_un(name), shape, dt))[:]
        ident, ones_bf, ones_f, _ = build_ident(r, sb_)
        hown = sb_("hown", [128, 8 * TOK], BF16)
        attn_f = hown
        wqT_b = hown[:, 0:8192]
        wo_b = hown[:, 8192:16384]
        wkvd_b = sb_("wkvd_b", [128, 8 * 128], BF16)
        wkupT_b = sb_("wkupT_b", [128, 32], BF16)
        wvupT_b = sb_("wvupT_b", [128, 32], BF16)
        wq_abs = sb_("wq_abs", [128, 8 * 256], BF16)
        wo_abs = sb_("wo_abs", [128, 2 * 1024], BF16)
        k_lat = sb_("k_lat", [128, 2 * SEQ], BF16)
        v_aug = sb_("v_aug", [128, 64 * 2 * 33], BF16)
        h_fmb = sb_("h_fmb", [128, 8 * 1024], BF16)
        qabs = sb_("qabs", [128, 3 * TOK], BF16)
        ylat = sb_("ylat", [128, 2 * TOK], BF16)
        cmask = sb_("cmask", [128, CMW], BF16)
        probs = sb_("probs", [128, 4 * 512], BF16)
        probs2 = sb_("probs2", [128, 4 * 512], BF16)
        h_bf = sb_("h_bf", [128, 1024], BF16)
        kv_st = sb_("kv_st", [128, 1024], BF16)
        sqs = hown[:, 0:6144]
        sqk = hown[:, 6144:14336]
        stat = sb_("stat", [128, 64], F32)
        bias_q = sb_("bias_q", [128, 32], F32)
        recb = sb_("recb", [128, 4 * 512], F32)
        ysb = sb_("ysb", [128, 4 * 512], F32)
        ynst = sb_("ynst", [128, 512], BF16)
        xt = sb_("xt", [128, 1024], F32)
        res = sb_("res", [128, 1024], F32)
        p_s = [ctx.enter_context(nc.psum_tensor(f"p_s{i}", [128, 512], F32)) for i in range(4)]
        p_y = [ctx.enter_context(nc.psum_tensor(f"p_y{i}", [128, 512], F32)) for i in range(4)]
        pp = PsumPool(p_s)
        ppsc = pp
        dcm = r.dma("sync", cmask[:], cm_d)
        r.dwait("vector", "sync", dcm)

        for i in range(8):
            r.dma("gpsimd", wqT_b[:, i * 1024:(i + 1) * 1024], wqT_d[i * 128:(i + 1) * 128, :])
            r.dma("gpsimd", wo_b[:, i * 1024:(i + 1) * 1024], wo_d[i * 128:(i + 1) * 128, :])
        for i in range(8):
            r.dma("gpsimd", wkvd_b[:, i * 128:(i + 1) * 128], wkvd_d[i * 128:(i + 1) * 128, :])
        r.dma("gpsimd", wkupT_b[:], wkupT_d)
        dw = r.dma("gpsimd", wvupT_b[:], wvupT_d)
        r.dwait("tensor", "gpsimd", dw)

        for hr in range(8):
            for cc in range(8):
                t, slot = ppsc.acquire(r, "tensor")
                mm = r.op("tensor", lambda e, hr=hr, cc=cc, t=t:
                          e.matmul(t[:, 0:32], wqT_b[:, hr * 1024 + cc * 128: hr * 1024 + (cc + 1) * 128],
                                   wkupT_b[:], start=True, stop=True))
                r.wait("scalar", "tensor", mm)
                ec = r.op("scalar", lambda e, hr=hr, cc=cc, t=t:
                          e.copy(out=wq_abs[:, cc * 256 + hr * 32: cc * 256 + hr * 32 + 32], in_=t[:, 0:32]))
                PsumPool.release(slot, "scalar", ec)
        for hr in range(8):
            for nb in range(2):
                t, slot = ppsc.acquire(r, "tensor")
                mm = r.op("tensor", lambda e, hr=hr, nb=nb, t=t:
                          e.matmul(t[0:32, :], wvupT_b[:], wo_b[:, hr * 1024 + nb * 512: hr * 1024 + (nb + 1) * 512],
                                   start=True, stop=True))
                r.wait("scalar", "tensor", mm)
                ec = r.op("scalar", lambda e, t=t: e.copy(out=kv_st[0:32, 0:512], in_=t[0:32, :]))
                r.wait("gpsimd", "scalar", ec)
                dmn = r.dma("gpsimd", wo_abs[32 * (hr % 4):32 * (hr % 4) + 32,
                                             (hr // 4) * 1024 + nb * 512:(hr // 4) * 1024 + (nb + 1) * 512],
                            kv_st[0:32, 0:512])
                r.dwait("scalar", "gpsimd", dmn)
                PsumPool.release(slot, "scalar", ec)

        xt2 = sb_("xt2", [128, 2 * 1024], F32)
        hb2 = sb_("hb2", [128, 2 * 1024], BF16)
        scr2 = [dict(sq=sb_("sq2a", [128, 1024], F32), st=stat[:, 8:16]),
                dict(sq=sb_("sq2b", [128, 1024], F32), st=stat[:, 16:24])]
        lnhist = []

        def ln_tile(src_ap, dsts):
            i = len(lnhist)
            w = i % 2
            gv, gt = lnhist[i - 2] if i >= 2 else (0, 0)
            r.wait("sync", "vector", gv)
            dn = r.dma("sync", xt2[:, w * 1024:(w + 1) * 1024], src_ap)
            r.dwait("vector", "sync", dn)
            r.dwait("scalar", "sync", dn)
            r.wait("vector", "tensor", gt)
            vc = emit_ln(r, xt2[:, w * 1024:(w + 1) * 1024], hb2[:, w * 1024:(w + 1) * 1024],
                         scr2[w], swait=gv)
            r.wait("tensor", "vector", vc)
            pairs = [(hb2[:, w * 1024 + j * 128: w * 1024 + (j + 1) * 128], dsts[j]) for j in range(8)]
            transpose_tiles(r, ident, pairs, pp)
            lnhist.append((vc, r.cnt["tensor"]))

        vcm = r.op("vector", lambda e: e.memset(v_aug[:], 1.0))
        r.wait("tensor", "vector", vcm)
        for kb in range(8):
            for tsub in range(8):
                tt_g = kb * 8 + tsub
                ln_tile(xb_d[tt_g * 128:(tt_g + 1) * 128, :],
                        [h_fmb[:, j * 1024 + tsub * 128: j * 1024 + (tsub + 1) * 128] for j in range(8)])
            r.wait("tensor", "scalar", r.cnt["scalar"])
            for nb in range(2):
                t, slot = ppsc.acquire(r, "tensor")
                mm = None
                for kc in range(8):
                    mm = r.op("tensor", lambda e, kc=kc, nb=nb, t=t:
                              e.matmul(t[:, :], wkvd_b[:, kc * 128:(kc + 1) * 128],
                                       h_fmb[:, kc * 1024 + nb * 512: kc * 1024 + (nb + 1) * 512],
                                       start=(kc == 0), stop=(kc == 7)))
                r.wait("scalar", "tensor", mm)
                ec = r.op("scalar", lambda e, t=t: e.copy(out=kv_st[:, 0:512], in_=t[:, :]))
                r.wait("gpsimd", "scalar", ec)
                dl = None
                for h in range(2):
                    for rep in range(3):
                        dl = r.dma("gpsimd", k_lat[32 * rep:32 * rep + 32,
                                                   h * SEQ + kb * 1024 + nb * 512: h * SEQ + kb * 1024 + (nb + 1) * 512],
                                   kv_st[32 * h:32 * h + 32, 0:512])
                r.dwait("tensor", "gpsimd", dl)
                r.dwait("scalar", "gpsimd", dl)
                for j in range(4):
                    kt_g = kb * 8 + nb * 4 + j
                    t2, slot2 = pp.acquire(r, "tensor")
                    t2b = t2[:].bitcast(BF16)
                    tc = r.op("tensor", lambda e, t2b=t2b, j=j: e.transpose(
                        t2b[:, 0:64], kv_st[64:128, j * 128:(j + 1) * 128], ident[64:128, 64:128]))
                    r.wait("vector", "tensor", tc)
                    v1 = r.op("vector", lambda e, t2b=t2b, kt_g=kt_g: e.tensor_copy(
                        out=v_aug[:, (kt_g * 2 + 0) * 33:(kt_g * 2 + 0) * 33 + 32], in_=t2b[:, 0:32]))
                    v2 = r.op("vector", lambda e, t2b=t2b, kt_g=kt_g: e.tensor_copy(
                        out=v_aug[:, (kt_g * 2 + 1) * 33:(kt_g * 2 + 1) * 33 + 32], in_=t2b[:, 32:64]))
                    PsumPool.release(slot2, "vector", v2)
                PsumPool.release(slot, "vector", r.cnt["vector"])

        for tt in range(16):
            ln_tile(xo_d[tt * 128:(tt + 1) * 128, :],
                    [hown[:, j * TOK + tt * 128: j * TOK + (tt + 1) * 128] for j in range(8)])
        r.wait("tensor", "scalar", r.cnt["scalar"])
        for ch in range(3):
            mwid = 96 if ch < 2 else 64
            for nb in range(4):
                t, slot = ppsc.acquire(r, "tensor")
                mm = None
                for cc in range(8):
                    mm = r.op("tensor", lambda e, ch=ch, mwid=mwid, nb=nb, cc=cc, t=t:
                              e.matmul(t[0:mwid, :], wq_abs[:, cc * 256 + ch * 96: cc * 256 + ch * 96 + mwid],
                                       hown[:, cc * TOK + nb * 512: cc * TOK + (nb + 1) * 512],
                                       start=(cc == 0), stop=(cc == 7)))
                r.wait("scalar", "tensor", mm)
                ec = r.op("scalar", lambda e, ch=ch, mwid=mwid, nb=nb, t=t: e.copy(
                    out=qabs[0:mwid, ch * TOK + nb * 512: ch * TOK + (nb + 1) * 512], in_=t[0:mwid, :]))
                PsumPool.release(slot, "scalar", ec)

        sq1 = r.op("scalar", lambda e: e.activation(out=sqs[:], in_=qabs[:], func=AF.Square))
        for h in range(2):
            r.wait("scalar", "tensor", r.cnt["tensor"])
            sq2 = r.op("scalar", lambda e, h=h: e.activation(out=sqk[0:32, 0:SEQ],
                                                             in_=k_lat[0:32, h * SEQ:(h + 1) * SEQ],
                                                             func=AF.Square))
            r.wait("tensor", "scalar", sq2)
            for i in range(16):
                t, slot = ppsc.acquire(r, "tensor")
                mm = r.op("tensor", lambda e, h=h, i=i, t=t: e.matmul(
                    t[0:1, :], ones_bf[0:32, 0:1],
                    sqk[0:32, i * 512:(i + 1) * 512], start=True, stop=True))
                r.wait("vector", "tensor", mm)
                v1 = r.op("vector", lambda e, t=t: e.reduce_max(out=stat[0:1, 40:41], in_=t[0:1, :], axis=AX.X))
                if i == 0:
                    v2 = r.op("vector", lambda e, h=h: e.tensor_copy(out=stat[0:1, 32 + h:33 + h],
                                                                     in_=stat[0:1, 40:41]))
                else:
                    v2 = r.op("vector", lambda e, h=h: e.tensor_tensor(out=stat[0:1, 32 + h:33 + h],
                                                                       in0=stat[0:1, 32 + h:33 + h],
                                                                       in1=stat[0:1, 40:41], op=ALU.max))
                PsumPool.release(slot, "vector", v2)
        for hr in range(8):
            for qb in range(4):
                t, slot = ppsc.acquire(r, "tensor")
                b0 = 32 * (hr % 3)
                mm = r.op("tensor", lambda e, hr=hr, qb=qb, t=t, b0=b0: e.matmul(
                    t[0:1, :], ones_bf[b0:b0 + 32, 0:1],
                    sqs[b0:b0 + 32, (hr // 3) * TOK + qb * 512:(hr // 3) * TOK + (qb + 1) * 512],
                    start=True, stop=True))
                r.wait("vector", "tensor", mm)
                r.op("vector", lambda e, t=t: e.reduce_max(out=stat[0:1, 41:42], in_=t[0:1, :], axis=AX.X))
                r.op("vector", lambda e, hr=hr: e.tensor_tensor(out=stat[0:1, 42:43], in0=stat[0:1, 41:42],
                                                                in1=stat[0:1, 32 + hr // 4:33 + hr // 4],
                                                                op=ALU.mult))
                vs = r.op("vector", lambda e: e.tensor_copy(out=stat[0:1, 43:44], in_=stat[0:1, 42:43]))
                r.wait("scalar", "vector", vs)
                ss = r.op("scalar", lambda e: e.activation(out=stat[0:1, 44:45], in_=stat[0:1, 43:44],
                                                           func=AF.Sqrt))
                r.wait("vector", "scalar", ss)
                vb = r.op("vector", lambda e: e.tensor_scalar(out=stat[0:1, 45:46], in0=stat[0:1, 44:45],
                                                              scalar1=-SCL * 1.02, scalar2=-1.0,
                                                              op0=ALU.mult, op1=ALU.add))
                r.wait("tensor", "vector", vb)
                mm2 = r.op("tensor", lambda e, t=t: e.matmul(t[:, 0:1], ones_f[0:1, :], stat[0:1, 45:46],
                                                             start=True, stop=True))
                r.wait("vector", "tensor", mm2)
                ve = r.op("vector", lambda e, hr=hr, qb=qb, t=t: e.tensor_copy(
                    out=bias_q[:, hr * 4 + qb: hr * 4 + qb + 1], in_=t[:, 0:1]))
                PsumPool.release(slot, "vector", ve)

        yslot_cons = [None] * 4
        it = [0]
        pend = []
        pool_cons = [0, 0, 0, 0]
        probs_w = [0, 0, 0, 0]
        prev_dmy = None
        for qb in range(4):
            for half in range(2):
                for kb in range(8):
                    for hri in range(4):
                        hr = half * 4 + hri
                        h = hr // 4
                        b0 = 32 * (hr % 3)
                        for kt in range(8):
                            kt_g = kb * 8 + kt
                            i = it[0]
                            it[0] += 1
                            pi = i % 4
                            t, slot = ppsc.acquire(r, "tensor")
                            mm = r.op("tensor", lambda e, t=t, h=h, b0=b0, hr=hr, qb=qb, kt_g=kt_g: e.matmul(
                                t[:, :], k_lat[b0:b0 + 32, h * SEQ + kt_g * 128: h * SEQ + (kt_g + 1) * 128],
                                qabs[b0:b0 + 32, (hr // 3) * TOK + qb * 512:(hr // 3) * TOK + (qb + 1) * 512],
                                start=True, stop=True))
                            while len(pend) >= 4:
                                pend.pop(0)()
                            off = qb * 512 - kt_g * 128 + 8064
                            r.wait("scalar", "tensor", mm)
                            r.wait("scalar", "vector", pool_cons[pi])
                            pe = r.op("scalar", lambda e, hr=hr, qb=qb, t=t, pi=pi: e.activation(
                                out=probs[:, pi * 512:(pi + 1) * 512], in_=t[:, :],
                                func=AF.Exp, scale=SCL,
                                bias=bias_q[:, hr * 4 + qb: hr * 4 + qb + 1]))
                            PsumPool.release(slot, "scalar", pe)
                            r.wait("vector", "scalar", pe)
                            r.wait("vector", "tensor", probs_w[pi])
                            pmu = r.op("vector", lambda e, pi=pi, off=off: e.tensor_tensor(
                                out=probs2[:, pi * 512:(pi + 1) * 512],
                                in0=probs[:, pi * 512:(pi + 1) * 512],
                                in1=cmask[:, off:off + 512], op=ALU.mult))
                            pool_cons[pi] = pmu

                            def mk_pv(hri=hri, h=h, kt_g=kt_g, kb=kb, kt=kt, pi=pi, pmu=pmu):
                                first = (kb == 0 and kt == 0)
                                lastmm = (kb == 7 and kt == 7)

                                def emit():
                                    r.wait("tensor", "vector", pmu)
                                    if first:
                                        r.wait("tensor", "vector", yslot_cons[hri])
                                    c = r.op("tensor", lambda e: e.matmul(
                                        p_y[hri][0:33, :], v_aug[:, (kt_g * 2 + h) * 33:(kt_g * 2 + h) * 33 + 33],
                                        probs2[:, pi * 512:(pi + 1) * 512], start=first, stop=lastmm))
                                    probs_w[pi] = c
                                return emit
                            pend.append(mk_pv())
                while pend:
                    pend.pop(0)()
                last_pv = r.cnt["tensor"]
                for hri in range(4):
                    r.wait("vector", "tensor", last_pv)
                    ev = r.op("vector", lambda e, hri=hri: e.tensor_copy(
                        out=ysb[0:33, hri * 512:(hri + 1) * 512], in_=p_y[hri][0:33, :]))
                    yslot_cons[hri] = ev
                for hri in range(4):
                    hr = half * 4 + hri
                    rb = recb[:, hri * 512:(hri + 1) * 512]
                    v1 = r.op("vector", lambda e, hri=hri, rb=rb: e.reciprocal(
                        out=rb[32:33, :], in_=ysb[32:33, hri * 512:(hri + 1) * 512]))
                    r.wait("sync", "vector", v1)
                    dprev = r.dma("sync", rb[0:1, 0:512], rb[32:33, 0:512])
                    for lo, n in ((1, 1), (2, 2), (4, 4), (8, 8), (16, 16)):
                        r.dwait("sync", "sync", dprev)
                        dprev = r.dma("sync", rb[lo:lo + n, 0:512], rb[0:n, 0:512])
                    r.dwait("vector", "sync", dprev)
                    r.dwait("vector", "sync", prev_dmy)
                    vn = r.op("vector", lambda e, hri=hri, rb=rb: e.tensor_tensor(
                        out=ynst[0:32, :], in0=ysb[0:32, hri * 512:(hri + 1) * 512],
                        in1=rb[0:32, :], op=ALU.mult))
                    r.wait("sync", "vector", vn)
                    prev_dmy = r.dma("sync", ylat[32 * (hr % 4):32 * (hr % 4) + 32,
                                                  (hr // 4) * TOK + qb * 512:(hr // 4) * TOK + (qb + 1) * 512],
                                     ynst[0:32, 0:512])
        dly = r.dcnt["sync"]
        r.dwait("tensor", "sync", dly)
        for cc in range(8):
            for nb in range(4):
                t, slot = ppsc.acquire(r, "tensor")
                mm = None
                for m in range(2):
                    mm = r.op("tensor", lambda e, cc=cc, nb=nb, m=m, t=t: e.matmul(
                        t[:, :], wo_abs[:, m * 1024 + cc * 128: m * 1024 + (cc + 1) * 128],
                        ylat[:, m * TOK + nb * 512: m * TOK + (nb + 1) * 512],
                        start=(m == 0), stop=(m == 1)))
                r.wait("scalar", "tensor", mm)
                ec = r.op("scalar", lambda e, cc=cc, nb=nb, t=t: e.copy(
                    out=attn_f[:, cc * TOK + nb * 512: cc * TOK + (nb + 1) * 512], in_=t[:, :]))
                PsumPool.release(slot, "scalar", ec)
        sdone = r.cnt["scalar"]
        r.wait("tensor", "scalar", sdone)
        prev_store = 0
        for tt in range(16):
            dn = r.dma("sync", xt[:], xo_d[tt * 128:(tt + 1) * 128, :])
            pairs = [(attn_f[:, j * TOK + tt * 128: j * TOK + (tt + 1) * 128],
                      h_bf[:, j * 128:(j + 1) * 128]) for j in range(8)]
            transpose_tiles(r, ident, pairs, pp, eng_evac="vector")
            r.dwait("vector", "sync", dn)
            r.dwait("vector", "sync", prev_store)
            va = r.op("vector", lambda e: e.tensor_tensor(out=res[:], in0=xt[:], in1=h_bf[:], op=ALU.add))
            r.wait("sync", "vector", va)
            prev_store = r.dma("sync", o_d[tt * 128:(tt + 1) * 128, :], res[:])
        r.dwait("sync", "sync", r.dcnt["sync"])


def build_l0_attn():
    nc = bass.Bass("TRN2", target_bir_lowering=False)
    xb_d = nc.dram_tensor("xb", [SEQ, D], F32, kind="ExternalInput").ap()
    xo_d = nc.dram_tensor("xo", [TOK, D], F32, kind="ExternalInput").ap()
    mk_d = nc.dram_tensor("cmask", [128, CMW], BF16, kind="ExternalInput").ap()
    wqT_d = nc.dram_tensor("wqT", [D, D], F32, kind="ExternalInput").ap()
    wkvd_d = nc.dram_tensor("wkvd", [D, 128], F32, kind="ExternalInput").ap()
    wkupT_d = nc.dram_tensor("wkupT", [128, 32], F32, kind="ExternalInput").ap()
    wvupT_d = nc.dram_tensor("wvupT", [128, 32], F32, kind="ExternalInput").ap()
    wo_d = nc.dram_tensor("wo", [D, D], F32, kind="ExternalInput").ap()
    o_d = nc.dram_tensor("o_x1", [TOK, D], F32, kind="ExternalOutput").ap()
    with ExitStack() as ctx:
        r = Rec(nc, ctx)
        with ExitStack() as s:
            emit_l0_attn(nc, r, s, xb_d, xo_d, mk_d, wqT_d, wkvd_d, wkupT_d, wvupT_d, wo_d, o_d)
        blk = ctx.enter_context(nc.Block())
        r.finalize(blk)
    return nc


def emit_l1_attn(nc, r, ctx, x2o_d, sel_d, ccin_d, ccout_d, wvT_d, wfcT_d, bfc_d, gam_d, bet_d,
                 scr_dram, o_writes):
    if True:
        sb_ = lambda name, shape, dt=F32: ctx.enter_context(nc.sbuf_tensor(_un(name), shape, dt))[:]
        ident, ones_bf, ones_f, _ = build_ident(r, sb_)
        wv_f = sb_("wv_f", [128, 8 * 1024], F32)
        wfc_f = sb_("wfc_f", [128, 8 * 1024], F32)
        ccsb = sb_("ccsb", [16, 1024], F32)
        selsb = sb_("selsb", [16, 2], F32)
        x2l = sb_("x2l", [1, 1024], F32)
        gam_r = sb_("gam_r", [128, 1024], F32)
        bet_r = sb_("bet_r", [128, 1024], F32)
        bfc_c = sb_("bfc_c", [128, 8], F32)
        h_bf = sb_("h_bf", [128, 1024], BF16)
        xt = sb_("xt", [128, 1024], F32)
        u_f = sb_("u_f", [128, 1024], F32)
        h2_f = sb_("h2_f", [128, 1024], F32)
        res = sb_("res", [128, 1024], F32)
        hrow = sb_("hrow", [128, 4096], F32)
        hcol = sb_("hcol", [128, 8], F32)
        vcol = sb_("vcol", [128, 8], F32)
        o2col = sb_("o2col", [128, 8], F32)
        o2rep = sb_("o2rep", [128, 1024], F32)
        stat = sb_("stat", [128, 24], F32)
        scr = dict(sq=sb_("sq", [128, 1024], F32), st=stat[:, 0:8])
        p_a = ctx.enter_context(nc.psum_tensor(_un("p_a"), [128, 512], F32))
        p_b = ctx.enter_context(nc.psum_tensor(_un("p_b"), [128, 512], F32))
        p_c = ctx.enter_context(nc.psum_tensor(_un("p_c"), [128, 512], F32))

        for i in range(8):
            r.dma("sync", wv_f[:, i * 1024:(i + 1) * 1024], wvT_d[i * 128:(i + 1) * 128, :])
            r.dma("sync", wfc_f[:, i * 1024:(i + 1) * 1024], wfcT_d[i * 128:(i + 1) * 128, :])
        dv = r.dma("sync", hrow[0:1, 1024:2048], gam_d[None, :])
        dv = r.dma("sync", hrow[0:1, 2048:3072], bet_d[None, :])
        dvb = r.dma("sync", bfc_c[:], bfc_d.rearrange("(c p) -> p c", p=128), allow_slow_non_contiguous=True)
        r.dma("sync", selsb[:], sel_d)
        r.dwait("tensor", "sync", dvb)
        for (row, dst) in ((1, gam_r), (2, bet_r)):
            for nb in range(2):
                mm = r.op("tensor", lambda e, row=row, nb=nb: e.matmul(
                    (p_a if nb == 0 else p_b)[:, :], ones_f[0:1, :],
                    hrow[0:1, row * 1024 + nb * 512: row * 1024 + (nb + 1) * 512], start=True, stop=True))
                r.wait("vector", "tensor", mm)
                ve = r.op("vector", lambda e, dst=dst, nb=nb: e.tensor_copy(
                    out=dst[:, nb * 512:(nb + 1) * 512], in_=(p_a if nb == 0 else p_b)[:, :]))
        vinit = r.cnt["vector"]
        r.wait("tensor", "vector", vinit)

        xt2l = sb_("xt2l", [128, 2 * 1024], F32)
        hb2l = sb_("hb2l", [128, 2 * 1024], BF16)
        scr2l = [dict(sq=sb_("sq2c", [128, 1024], F32), st=stat[:, 8:16]),
                 dict(sq=sb_("sq2d", [128, 1024], F32), st=stat[:, 16:24])]
        cs_hist = []
        for tt in range(16):
            w = tt % 2
            gv, gt = cs_hist[tt - 2] if tt >= 2 else (0, 0)
            r.wait("sync", "vector", gv)
            dn = r.dma("sync", xt2l[:, w * 1024:(w + 1) * 1024], x2o_d[tt * 128:(tt + 1) * 128, :])
            r.dwait("vector", "sync", dn)
            r.dwait("scalar", "sync", dn)
            r.wait("vector", "tensor", gt)
            vc = emit_ln(r, xt2l[:, w * 1024:(w + 1) * 1024], hb2l[:, w * 1024:(w + 1) * 1024],
                         scr2l[w], swait=gv)
            r.wait("tensor", "vector", vc)
            for nb in range(2):
                r.op("tensor", lambda e, nb=nb, tt=tt, w=w: e.matmul(
                    (p_a if nb == 0 else p_b)[0:1, :], ones_bf[:],
                    hb2l[:, w * 1024 + nb * 512: w * 1024 + (nb + 1) * 512],
                    start=(tt == 0), stop=(tt == 15)))
            cs_hist.append((vc, r.cnt["tensor"]))
        mmc = r.cnt["tensor"]
        r.wait("vector", "tensor", mmc)
        r.op("vector", lambda e: e.tensor_copy(out=hrow[0:1, 0:512], in_=p_a[0:1, :]))
        vh = r.op("vector", lambda e: e.tensor_copy(out=hrow[0:1, 512:1024], in_=p_b[0:1, :]))
        r.wait("sync", "vector", vh)
        d1 = r.dma("sync", x2l[0:1, :], x2o_d[2047:2048, :])
        r.dma("sync", ccin_d[1:2, :], hrow[0:1, 0:1024])
        r.dwait("sync", "sync", d1)
        dci = r.dma("sync", ccin_d[0:1, :], x2l[0:1, :])
        r.dwait("gpsimd", "sync", dci)
        ncc = r.cc(lambda e: e.collective_compute(
            "AllGather", ALU.bypass, replica_groups=[list(range(8))],
            ins=[ccin_d.opt()], outs=[ccout_d.opt()]))
        r.cwait("sync", ncc)
        dcg = r.dma("sync", ccsb[:], ccout_d)
        r.dwait("tensor", "sync", dcg)
        for nb in range(2):
            mm = r.op("tensor", lambda e, nb=nb: e.matmul(
                p_a[0:1, :], selsb[:, 0:1], ccsb[:, nb * 512:(nb + 1) * 512],
                start=True, stop=True))
            mm2 = r.op("tensor", lambda e, nb=nb: e.matmul(
                p_b[0:1, :], selsb[:, 1:2], ccsb[:, nb * 512:(nb + 1) * 512],
                start=True, stop=True))
            r.wait("vector", "tensor", mm2)
            r.op("vector", lambda e, nb=nb: e.tensor_copy(
                out=x2l[0:1, nb * 512:(nb + 1) * 512], in_=p_a[0:1, :]))
            vh = r.op("vector", lambda e, nb=nb: e.tensor_copy(
                out=hrow[0:1, nb * 512:(nb + 1) * 512], in_=p_b[0:1, :]))
            r.wait("tensor", "vector", vh)
        x2l_done = vh
        r.wait("sync", "vector", vh)
        r.dma("sync", scr_dram.ap()[0:8, :].flatten().rearrange("(a p) -> a p", a=1), hrow[0:1, 0:1024])
        r.dwait("sync", "sync", r.dcnt["sync"])
        r.dma("sync", hcol[:], scr_dram.ap()[0:8, :].rearrange("a p -> p a"),
              allow_slow_non_contiguous=True)
        r.dwait("tensor", "sync", r.dcnt["sync"])
        for ec_ in range(8):
            mm = None
            for kc in range(8):
                mm = r.op("tensor", lambda e, ec_=ec_, kc=kc: e.matmul(
                    p_c[:, 0:1], wv_f[:, kc * 1024 + ec_ * 128: kc * 1024 + (ec_ + 1) * 128],
                    hcol[:, kc:kc + 1], start=(kc == 0), stop=(kc == 7)))
            r.wait("vector", "tensor", mm)
            ve = r.op("vector", lambda e, ec_=ec_: e.tensor_copy(out=vcol[:, ec_:ec_ + 1], in_=p_c[:, 0:1]))
            r.wait("tensor", "vector", ve)
        for ec_ in range(8):
            mm = None
            for kc in range(8):
                mm = r.op("tensor", lambda e, ec_=ec_, kc=kc: e.matmul(
                    p_c[:, 0:1], wfc_f[:, kc * 1024 + ec_ * 128: kc * 1024 + (ec_ + 1) * 128],
                    vcol[:, kc:kc + 1], start=(kc == 0), stop=(kc == 7)))
            r.wait("vector", "tensor", mm)
            ve = r.op("vector", lambda e, ec_=ec_: e.tensor_tensor(
                out=o2col[:, ec_:ec_ + 1], in0=p_c[:, 0:1], in1=bfc_c[:, ec_:ec_ + 1], op=ALU.add))
            r.wait("tensor", "vector", ve)
        vo = r.cnt["vector"]
        r.wait("sync", "vector", vo)
        r.dma("sync", scr_dram.ap()[8:16, :].rearrange("a p -> p a"), o2col[:],
              allow_slow_non_contiguous=True)
        r.dwait("sync", "sync", r.dcnt["sync"])
        dn = r.dma("sync", hrow[0:1, 3072:4096], scr_dram.ap()[8:16, :].flatten().rearrange("(a p) -> a p", a=1))
        r.dwait("tensor", "sync", dn)
        for nb in range(2):
            mm = r.op("tensor", lambda e, nb=nb: e.matmul(
                (p_a if nb == 0 else p_b)[:, :], ones_f[0:1, :],
                hrow[0:1, 3072 + nb * 512: 3072 + (nb + 1) * 512], start=True, stop=True))
            r.wait("vector", "tensor", mm)
            r.op("vector", lambda e, nb=nb: e.tensor_copy(
                out=o2rep[:, nb * 512:(nb + 1) * 512], in_=(p_a if nb == 0 else p_b)[:, :]))

        prev_store = 0
        for tt in range(17):
            if tt == 0:
                r.wait("vector", "scalar", r.cnt["scalar"])
                vz = r.op("vector", lambda e: e.memset(xt[:], 0.0))
                r.wait("sync", "vector", vz)
                dn = r.dma("sync", xt[127:128, :], x2l[0:1, :])
            else:
                dn = r.dma("sync", xt[:], x2o_d[(tt - 1) * 128: tt * 128, :])
            r.dwait("vector", "sync", dn)
            r.dwait("scalar", "sync", dn)
            emit_ln(r, xt[:], u_f[:], scr)
            r.op("vector", lambda e: e.tensor_tensor(out=h2_f[:], in0=u_f[:], in1=o2rep[:], op=ALU.add))
            emit_ln(r, h2_f[:], u_f[:], scr)
            r.op("vector", lambda e: e.tensor_tensor(out=u_f[:], in0=u_f[:], in1=gam_r[:], op=ALU.mult))
            r.op("vector", lambda e: e.tensor_tensor(out=u_f[:], in0=u_f[:], in1=bet_r[:], op=ALU.add))
            r.dwait("vector", "sync", prev_store)
            va2 = r.op("vector", lambda e: e.tensor_tensor(out=res[:], in0=xt[:], in1=u_f[:], op=ALU.add))
            r.wait("sync", "vector", va2)
            prev_store = r.dma("sync", o_writes(tt), res[:])
        r.dwait("sync", "sync", r.dcnt["sync"])




def emit_l1_ffn(nc, r, ctx, x3_d, wrT_d, wkT_d, wvT_d, mur_d, muk_d, o_d, pm_d=None):
    NT = 2176
    if True:
        sb_ = lambda name, shape, dt=F32: ctx.enter_context(nc.sbuf_tensor(_un(name), shape, dt))[:]
        ident, ones_bf, ones_f, _ = build_ident(r, sb_)
        wr_b = sb_("wr_b", [128, 8 * 1024], BF16)
        wk_b = sb_("wk_b", [128, 8 * 1024], BF16)
        wv_b = sb_("wv_b", [128, 8 * 1024], BF16)
        h_fm = sb_("h_fm", [128, 8 * NT], BF16)
        mixf = sb_("mixf", [128, 8 * TOK], BF16)
        r_bf = sb_("r_bf", [128, 8 * TOK], BF16)
        kact = sb_("kact", [128, 8 * TOK], BF16)
        outf = mixf
        mu_sb = sb_("mu_sb", [128, 8], F32)
        muk_sb = sb_("muk_sb", [128, 8], F32)
        om_sb = sb_("om_sb", [128, 8], F32)
        omk_sb = sb_("omk_sb", [128, 8], F32)
        h_bf = sb_("h_bf", [128, 1024], BF16)
        krel = sb_("krel", [128, 512], F32)
        xt = sb_("xt", [128, 1024], F32)
        res = sb_("res", [128, 1024], F32)
        stat = sb_("stat", [128, 24], F32)
        scr = dict(sq=sb_("sq", [128, 1024], F32), st=stat[:, 0:8])
        p_a = ctx.enter_context(nc.psum_tensor(_un("p_a"), [128, 512], F32))
        p_b = ctx.enter_context(nc.psum_tensor(_un("p_b"), [128, 512], F32))
        p_t0 = ctx.enter_context(nc.psum_tensor(_un("p_t0"), [128, 512], F32))
        p_t1 = ctx.enter_context(nc.psum_tensor(_un("p_t1"), [128, 512], F32))
        pp = PsumPool([p_t0, p_t1])
        ppsc = PsumPool([p_a, p_b])
        pm_sb = sb_("pm_sb", [128, 1], F32)

        for i in range(8):
            r.dma("gpsimd", wr_b[:, i * 1024:(i + 1) * 1024], wrT_d[i * 128:(i + 1) * 128, :])
            r.dma("gpsimd", wk_b[:, i * 1024:(i + 1) * 1024], wkT_d[i * 128:(i + 1) * 128, :])
            r.dma("gpsimd", wv_b[:, i * 1024:(i + 1) * 1024], wvT_d[i * 128:(i + 1) * 128, :])
        dm1 = r.dma("sync", mu_sb[:], mur_d.rearrange("(c p) -> p c", p=128), allow_slow_non_contiguous=True)
        dm2 = r.dma("sync", muk_sb[:], muk_d.rearrange("(c p) -> p c", p=128), allow_slow_non_contiguous=True)
        if pm_d is not None:
            dm2 = r.dma("sync", pm_sb[:], pm_d)
        r.dwait("vector", "sync", dm2)
        if pm_d is None:
            r.op("vector", lambda e: e.memset(pm_sb[:], 1.0))
        r.op("vector", lambda e: e.tensor_scalar(out=om_sb[:], in0=mu_sb[:], scalar1=-1.0, scalar2=1.0,
                                                 op0=ALU.mult, op1=ALU.add))
        r.op("vector", lambda e: e.tensor_scalar(out=omk_sb[:], in0=muk_sb[:], scalar1=-1.0, scalar2=1.0,
                                                 op0=ALU.mult, op1=ALU.add))

        xt2f = mixf[:, 0:4096].bitcast(F32)
        hb2f = mixf[:, 8192:10240]
        scr2f = [dict(sq=mixf[:, 4096:6144].bitcast(F32), st=stat[:, 0:8]),
                 dict(sq=mixf[:, 6144:8192].bitcast(F32), st=stat[:, 8:16])]
        lf_hist = []
        for tt in range(17):
            w = tt % 2
            gv, gt = lf_hist[tt - 2] if tt >= 2 else (0, 0)
            r.wait("sync", "vector", gv)
            dn = r.dma("sync", xt2f[:, w * 1024:(w + 1) * 1024], x3_d[tt * 128:(tt + 1) * 128, :])
            r.dwait("vector", "sync", dn)
            r.dwait("scalar", "sync", dn)
            r.wait("vector", "tensor", gt)
            vc = emit_ln(r, xt2f[:, w * 1024:(w + 1) * 1024], hb2f[:, w * 1024:(w + 1) * 1024],
                         scr2f[w], swait=gv)
            if tt == 0:
                vc = r.op("vector", lambda e: e.tensor_tensor(
                    out=hb2f[:, 0:1024], in0=hb2f[:, 0:1024],
                    in1=pm_sb[:].broadcast_to([128, 1024]), op=ALU.mult))
            r.wait("tensor", "vector", vc)
            pairs = [(hb2f[:, w * 1024 + j * 128: w * 1024 + (j + 1) * 128],
                      h_fm[:, j * NT + tt * 128: j * NT + (tt + 1) * 128]) for j in range(8)]
            transpose_tiles(r, ident, pairs, pp)
            lf_hist.append((vc, r.cnt["tensor"]))

        def mixes(mu, om, dst):
            r.wait("vector", "scalar", r.cnt["scalar"])
            for c in range(8):
                r.op("vector", lambda e, c=c, mu=mu: e.tensor_tensor(
                    out=dst[:, c * TOK:(c + 1) * TOK],
                    in0=h_fm[:, c * NT + 128: c * NT + 128 + TOK],
                    in1=mu[:, c:c + 1].broadcast_to([128, TOK]), op=ALU.mult))
                r.op("vector", lambda e, c=c, om=om: e.tensor_tensor(
                    out=kact[:, c * TOK:(c + 1) * TOK],
                    in0=h_fm[:, c * NT + 127: c * NT + 127 + TOK],
                    in1=om[:, c:c + 1].broadcast_to([128, TOK]), op=ALU.mult))
                r.op("vector", lambda e, c=c: e.tensor_tensor(
                    out=dst[:, c * TOK:(c + 1) * TOK], in0=dst[:, c * TOK:(c + 1) * TOK],
                    in1=kact[:, c * TOK:(c + 1) * TOK], op=ALU.add))
            return r.cnt["vector"]

        def proj(w_b, src, act):
            for ec_ in range(8):
                for nb in range(4):
                    t, slot = ppsc.acquire(r, "tensor")
                    mm = None
                    for kc in range(8):
                        mm = r.op("tensor", lambda e, ec_=ec_, nb=nb, kc=kc, t=t, w_b=w_b, src=src: e.matmul(
                            t[:, :], w_b[:, kc * 1024 + ec_ * 128: kc * 1024 + (ec_ + 1) * 128],
                            src[:, kc * TOK + nb * 512: kc * TOK + (nb + 1) * 512],
                            start=(kc == 0), stop=(kc == 7)))
                    cnt = act(ec_, nb, t, mm)
                    PsumPool.release(slot, "scalar" if cnt[0] == "s" else "vector", cnt[1])

        vmix = mixes(mu_sb, om_sb, mixf)
        r.wait("tensor", "vector", vmix)

        def act_sig(ec_, nb, t, mm):
            r.wait("scalar", "tensor", mm)
            c = r.op("scalar", lambda e, ec_=ec_, nb=nb, t=t: e.activation(
                out=r_bf[:, ec_ * TOK + nb * 512: ec_ * TOK + (nb + 1) * 512], in_=t[:, :], func=AF.Sigmoid))
            return ("s", c)
        proj(wr_b, mixf, act_sig)

        vmix = mixes(muk_sb, omk_sb, mixf)
        r.wait("tensor", "vector", vmix)

        def act_relu2(ec_, nb, t, mm):
            r.wait("scalar", "tensor", mm)
            c1 = r.op("scalar", lambda e, t=t: e.activation(out=krel[:, :], in_=t[:, :], func=AF.Relu))
            c2 = r.op("scalar", lambda e, ec_=ec_, nb=nb: e.activation(
                out=kact[:, ec_ * TOK + nb * 512: ec_ * TOK + (nb + 1) * 512], in_=krel[:, :], func=AF.Square))
            return ("s", c2)
        proj(wk_b, mixf, act_relu2)

        def act_vout(ec_, nb, t, mm):
            r.wait("vector", "tensor", mm)
            c = r.op("vector", lambda e, ec_=ec_, nb=nb, t=t: e.tensor_tensor(
                out=outf[:, ec_ * TOK + nb * 512: ec_ * TOK + (nb + 1) * 512], in0=t[:, :],
                in1=r_bf[:, ec_ * TOK + nb * 512: ec_ * TOK + (nb + 1) * 512], op=ALU.mult))
            return ("v", c)
        proj(wv_b, kact, act_vout)

        r.wait("tensor", "vector", r.cnt["vector"])
        prev_store = 0
        for tt in range(16):
            dn = r.dma("sync", xt[:], x3_d[128 + tt * 128: 128 + (tt + 1) * 128, :])
            pairs = [(outf[:, j * TOK + tt * 128: j * TOK + (tt + 1) * 128],
                      h_bf[:, j * 128:(j + 1) * 128]) for j in range(8)]
            transpose_tiles(r, ident, pairs, pp, eng_evac="vector")
            r.dwait("vector", "sync", dn)
            r.dwait("vector", "sync", prev_store)
            va = r.op("vector", lambda e: e.tensor_tensor(out=res[:], in0=xt[:], in1=h_bf[:], op=ALU.add))
            r.wait("sync", "vector", va)
            prev_store = r.dma("sync", o_d[tt * 128:(tt + 1) * 128, :], res[:])
        r.dwait("sync", "sync", r.dcnt["sync"])


def build_l1_ffn():
    nc = bass.Bass("TRN2", target_bir_lowering=False)
    x3_d = nc.dram_tensor("x3p", [2176, D], F32, kind="ExternalInput").ap()
    wrT_d = nc.dram_tensor("wrT", [D, D], F32, kind="ExternalInput").ap()
    wkT_d = nc.dram_tensor("wkT", [D, D], F32, kind="ExternalInput").ap()
    wvT_d = nc.dram_tensor("wvT", [D, D], F32, kind="ExternalInput").ap()
    mur_d = nc.dram_tensor("mur", [D], F32, kind="ExternalInput").ap()
    muk_d = nc.dram_tensor("muk", [D], F32, kind="ExternalInput").ap()
    o_d = nc.dram_tensor("o_x4", [TOK, D], F32, kind="ExternalOutput").ap()
    with ExitStack() as ctx:
        r = Rec(nc, ctx)
        with ExitStack() as s:
            emit_l1_ffn(nc, r, s, x3_d, wrT_d, wkT_d, wvT_d, mur_d, muk_d, o_d)
        blk = ctx.enter_context(nc.Block())
        r.finalize(blk)
    return nc


def barrier(r):
    snap = {e: r.cnt[e] for e in r.ENGS}
    dnap = {q: r.dcnt[q] for q in r.dsem}
    for e in r.ENGS:
        for p in r.ENGS:
            if p != e:
                r.wait(e, p, snap[p])
        for q in r.dsem:
            r.dwait(e, q, dnap[q])
        r.cwait(e, r.ccnt)


def build_full():
    nc = bass.Bass("TRN2", target_bir_lowering=False)
    xb_d = nc.dram_tensor("xb", [SEQ, D], F32, kind="ExternalInput").ap()
    xo_d = nc.dram_tensor("xo", [TOK, D], F32, kind="ExternalInput").ap()
    cm_d = nc.dram_tensor("cmask", [128, CMW], BF16, kind="ExternalInput").ap()
    wqT_d = nc.dram_tensor("wqT", [D, D], F32, kind="ExternalInput").ap()
    wkvd_d = nc.dram_tensor("wkvd", [D, 128], F32, kind="ExternalInput").ap()
    wkupT_d = nc.dram_tensor("wkupT", [128, 32], F32, kind="ExternalInput").ap()
    wvupT_d = nc.dram_tensor("wvupT", [128, 32], F32, kind="ExternalInput").ap()
    wo_d = nc.dram_tensor("wo", [D, D], F32, kind="ExternalInput").ap()
    wsl_d = nc.dram_tensor("wsl", [768, D], F32, kind="ExternalInput").ap()
    wdsl_d = nc.dram_tensor("wdsl", [128, 2816], F32, kind="ExternalInput").ap()
    wvT_d = nc.dram_tensor("wvT", [D, D], F32, kind="ExternalInput").ap()
    wfcT_d = nc.dram_tensor("wfcT", [D, D], F32, kind="ExternalInput").ap()
    bfc_d = nc.dram_tensor("bfc", [D], F32, kind="ExternalInput").ap()
    gam_d = nc.dram_tensor("gam", [D], F32, kind="ExternalInput").ap()
    bet_d = nc.dram_tensor("bet", [D], F32, kind="ExternalInput").ap()
    wrT_d = nc.dram_tensor("wrT", [D, D], F32, kind="ExternalInput").ap()
    wkT_d = nc.dram_tensor("wkT", [D, D], F32, kind="ExternalInput").ap()
    wvT2_d = nc.dram_tensor("wvT2", [D, D], F32, kind="ExternalInput").ap()
    mur_d = nc.dram_tensor("mur", [D], F32, kind="ExternalInput").ap()
    muk_d = nc.dram_tensor("muk", [D], F32, kind="ExternalInput").ap()
    pm_d = nc.dram_tensor("pm", [128, 1], F32, kind="ExternalInput").ap()
    sel_d = nc.dram_tensor("sel", [16, 2], F32, kind="ExternalInput").ap()
    o_d = nc.dram_tensor("o_x4", [TOK, D], F32, kind="ExternalOutput").ap()
    x1_dram = nc.dram_tensor("x1scr", [TOK, D], F32)
    x2_dram = nc.dram_tensor("x2scr", [TOK, D], F32)
    x3_dram = nc.dram_tensor("x3scr", [2176, D], F32)
    scr_dram = nc.dram_tensor("scrv", [16, 128], F32)
    ccin_dram = nc.dram_tensor("ccin", [2, D], F32)
    ccout_dram = nc.dram_tensor("ccout", [16, D], F32)
    wqin_dram = nc.dram_tensor("wqin", [128, 8960], BF16)
    wqout_dram = nc.dram_tensor("wqout", [1024, 8960], BF16)
    with ExitStack() as ctx:
        r = Rec(nc, ctx)
        with ExitStack() as s:
            emit_l0_attn(nc, r, s, xb_d, xo_d, cm_d, wqT_d, wkvd_d, wkupT_d, wvupT_d, wo_d,
                         x1_dram.ap())
        barrier(r)
        with ExitStack() as s:
            emit_l0_ffn(nc, r, s, x1_dram.ap(), wsl_d, wdsl_d, wqin_dram.ap(), wqout_dram.ap(),
                        x2_dram.ap())
        barrier(r)
        with ExitStack() as s:
            emit_l1_attn(nc, r, s, x2_dram.ap(), sel_d, ccin_dram.ap(), ccout_dram.ap(),
                         wvT_d, wfcT_d, bfc_d, gam_d, bet_d,
                         scr_dram, lambda tt: x3_dram.ap()[tt * 128:(tt + 1) * 128, :])
        barrier(r)
        with ExitStack() as s:
            emit_l1_ffn(nc, r, s, x3_dram.ap(), wrT_d, wkT_d, wvT2_d, mur_d, muk_d, o_d, pm_d=pm_d)
        blk = ctx.enter_context(nc.Block())
        r.finalize(blk)
    return nc


from concourse.bass_utils import run_bass_kernel_spmd

_NC_CACHE = {}


def _get(name, builder):
    if name not in _NC_CACHE:
        _NC_CACHE[name] = builder()
    return _NC_CACHE[name]


def kernel(**kw):
    import ml_dtypes
    f32 = np.float32
    g = lambda k: np.ascontiguousarray(np.asarray(kw[k], dtype=f32))
    x0 = g("L0_attn_x")
    cores = list(range(8))

    wq, wkd, wvd = g("L0_attn_wq"), g("L0_attn_wk_down"), g("L0_attn_wv_down")
    wku, wvu, wo = g("L0_attn_wk_up"), g("L0_attn_wv_up"), g("L0_attn_wo")
    wg_, wu_, wd_ = g("L0_ffn_w_gate"), g("L0_ffn_w_up"), g("L0_ffn_w_down")
    wqT = np.ascontiguousarray(wq.T)
    wkvd = np.ascontiguousarray(np.concatenate([wkd, wvd], axis=1))
    wkupT = np.ascontiguousarray(wku.T)
    wvupT = np.ascontiguousarray(wvu.T)
    wqkv, wfc = g("L1_attn_w_qkv"), g("L1_attn_w_fc")
    bfc, gam, bet = g("L1_attn_b_fc"), g("L1_attn_gamma"), g("L1_attn_beta")
    wvT = np.ascontiguousarray(wqkv[2048:3072].T)
    wfcT = np.ascontiguousarray(wfc.T)
    wr, wk_, wv_ = g("L1_ffn_W_r"), g("L1_ffn_W_k"), g("L1_ffn_W_v")
    mur, muk = g("L1_ffn_mu_r"), g("L1_ffn_mu_k")
    wrT = np.ascontiguousarray(wr.T)
    wkT = np.ascontiguousarray(wk_.T)
    wvT2 = np.ascontiguousarray(wv_.T)

    iu = np.arange(CMW)[None, :]
    pidx = np.arange(128)[:, None]
    w1024 = np.concatenate([wg_, wu_], axis=0).reshape(44, 128, 1024)
    in_maps = []
    for c in range(8):
        b, q0 = c // 4, (c % 4) * TOK
        cm = np.where(pidx <= (q0 + iu - 8064), 1.0, 0.0).astype(f32).astype(ml_dtypes.bfloat16)
        sel = np.zeros((16, 2), f32)
        if c % 4 != 0:
            sel[2 * (c - 1), 0] = 1.0
        for rr in range(4 * b, 4 * b + 4):
            sel[2 * rr + 1, 1] = 1.0
        pm = np.zeros((128, 1), f32) if c % 4 == 0 else np.ones((128, 1), f32)
        idxs = [(6 * c + s) if (6 * c + s) < 44 else 0 for s in range(6)]
        wsl = np.ascontiguousarray(w1024[idxs].reshape(768, 1024))
        wdsl = np.ascontiguousarray(wd_[c * 128:(c + 1) * 128, :])
        in_maps.append(dict(xb=x0[b], xo=x0[b, q0:q0 + TOK], cmask=cm,
                            wqT=wqT, wkvd=wkvd, wkupT=wkupT, wvupT=wvupT, wo=wo,
                            wsl=wsl, wdsl=wdsl,
                            wvT=wvT, wfcT=wfcT, bfc=bfc, gam=gam, bet=bet,
                            wrT=wrT, wkT=wkT, wvT2=wvT2, mur=mur, muk=muk,
                            pm=pm, sel=sel))
    res = run_bass_kernel_spmd(_get("FULL", build_full), in_maps, core_ids=cores)
    x4 = np.stack([np.concatenate([res.results[b * 4 + i]["o_x4"] for i in range(4)], 0)
                   for b in range(2)])
    return np.ascontiguousarray(x4.astype(np.float32))


# version 21
# speedup vs baseline: 1.0219x; 1.0219x over previous
import numpy as np
from contextlib import ExitStack

import concourse.bass as bass
import concourse.mybir as mybir

F32 = mybir.dt.float32
BF16 = mybir.dt.bfloat16
AF = mybir.ActivationFunctionType
ALU = mybir.AluOpType
AX = mybir.AxisListType

TOK = 2048
D = 1024
SEQ = 8192
NEG = -1e30


_NAME_N = [0]


def _un(name):
    _NAME_N[0] += 1
    return f"{name}_{_NAME_N[0]}"


class Rec:
    ENGS = ("sync", "tensor", "vector", "scalar", "gpsimd")

    def __init__(self, nc, ctx):
        self.nc = nc
        self.ops = {e: [] for e in self.ENGS}
        self.sem = {e: ctx.enter_context(nc.semaphore(f"s_{e}")) for e in self.ENGS}
        self.cnt = {e: 0 for e in self.ENGS}
        self.dsem = {q: ctx.enter_context(nc.semaphore(f"d_{q}")) for q in ("sync", "gpsimd")}
        self.dcnt = {q: 0 for q in ("sync", "gpsimd")}
        self.csem = ctx.enter_context(nc.semaphore("s_cc"))
        self.ccnt = 0

    def cc(self, make):
        self.ccnt += 1
        n = self.ccnt
        sem = self.csem
        self.ops["gpsimd"].append(lambda e, make=make, sem=sem: make(e).then_inc(sem, 1))
        return n

    def cwait(self, eng, val):
        if val is None or val <= 0:
            return
        sem = self.csem
        self.ops[eng].append(lambda e, sem=sem, val=val: e.wait_ge(sem, val))

    SELF_SYNC = ("vector", "scalar", "gpsimd")

    def op(self, eng, make):
        sem = self.sem[eng]
        if eng in self.SELF_SYNC and self.cnt[eng] > 0:
            prev = self.cnt[eng]
            self.ops[eng].append(lambda e, sem=sem, prev=prev: e.wait_ge(sem, prev))
        self.cnt[eng] += 1
        n = self.cnt[eng]
        self.ops[eng].append(lambda e, make=make, sem=sem: make(e).then_inc(sem, 1))
        return n

    def dma(self, q, out, in_, **kw):
        self.dcnt[q] += 16
        n = self.dcnt[q]
        sem = self.dsem[q]
        self.ops[q].append(
            lambda e, out=out, in_=in_, kw=kw, sem=sem: e.dma_start(out=out, in_=in_, **kw).then_inc(sem, 16))
        return n

    def wait(self, eng, prod, val):
        if val is None or val <= 0:
            return
        sem = self.sem[prod]
        self.ops[eng].append(lambda e, sem=sem, val=val: e.wait_ge(sem, val))

    def dwait(self, eng, q, val):
        if val is None or val <= 0:
            return
        sem = self.dsem[q]
        self.ops[eng].append(lambda e, sem=sem, val=val: e.wait_ge(sem, val))

    def finalize(self, blk):
        for eng in self.ENGS:
            fns = self.ops[eng]
            getattr(blk, eng)(lambda e, fns=fns: [f(e) for f in fns])


def build_ident(r, sb_):
    io = sb_("io_", [128, 128], F32)
    pi = sb_("pi_", [128, 1], F32)
    ident = sb_("ident_", [128, 128], BF16)
    ones_bf = sb_("onesb_", [128, 1], BF16)
    ones_f = sb_("onesf_", [128, 128], F32)
    r.op("gpsimd", lambda e: e.iota(io[:], pattern=[[1, 128]], base=0, channel_multiplier=0,
                                    allow_small_or_imprecise_dtypes=True))
    g2 = r.op("gpsimd", lambda e: e.iota(pi[:], pattern=[[0, 1]], base=0, channel_multiplier=1,
                                         allow_small_or_imprecise_dtypes=True))
    r.wait("vector", "gpsimd", g2)
    r.op("vector", lambda e: e.tensor_tensor(out=ident[:], in0=io[:], in1=pi[:].broadcast_to([128, 128]),
                                             op=ALU.is_equal))
    r.op("vector", lambda e: e.memset(ones_bf[:], 1.0))
    vc = r.op("vector", lambda e: e.memset(ones_f[0:1, :], 1.0))
    return ident, ones_bf, ones_f, vc


def emit_ln(r, x_ap, out_ap, scr, D_=1024, eps=1e-5, swait=None):
    sq, st = scr["sq"], scr["st"]
    s_sum, s_sq, s_m, s_var, s_rstd, s_tmp = (st[:, i:i + 1] for i in range(6))
    r.wait("scalar", "vector", r.cnt["vector"] if swait is None else swait)
    sc = r.op("scalar", lambda e: e.activation(out=sq[:, 0:D_], in_=x_ap, func=AF.Square))
    r.wait("vector", "scalar", sc)
    r.op("vector", lambda e: e.reduce_sum(out=s_sum, in_=x_ap, axis=AX.X))
    r.op("vector", lambda e: e.reduce_sum(out=s_sq, in_=sq[:, 0:D_], axis=AX.X))
    r.op("vector", lambda e: e.tensor_scalar(out=s_m, in0=s_sum, scalar1=1.0 / D_, scalar2=None, op0=ALU.mult))
    r.op("vector", lambda e: e.tensor_scalar(out=s_tmp, in0=s_sq, scalar1=1.0 / D_, scalar2=None, op0=ALU.mult))
    r.op("vector", lambda e: e.tensor_tensor(out=s_var, in0=s_m, in1=s_m, op=ALU.mult))
    r.op("vector", lambda e: e.tensor_tensor(out=s_var, in0=s_tmp, in1=s_var, op=ALU.subtract))
    vc = r.op("vector", lambda e: e.tensor_scalar(out=s_var, in0=s_var, scalar1=eps, scalar2=None, op0=ALU.add))
    r.wait("scalar", "vector", vc)
    sc2 = r.op("scalar", lambda e: e.activation(out=s_rstd, in_=s_var, func=AF.Sqrt))
    r.wait("vector", "scalar", sc2)
    r.op("vector", lambda e: e.reciprocal(out=s_rstd, in_=s_rstd))
    r.op("vector", lambda e: e.tensor_tensor(out=sq[:, 0:D_], in0=x_ap,
                                             in1=s_m.broadcast_to([128, D_]), op=ALU.subtract))
    vc2 = r.op("vector", lambda e: e.tensor_tensor(out=out_ap, in0=sq[:, 0:D_],
                                                   in1=s_rstd.broadcast_to([128, D_]), op=ALU.mult))
    return vc2


def emit_quant(r, x_ap, out_ap, scr, nelem, qb_ap=None):
    qb, qc, qbm, qsc, qr, sg = (qb_ap if qb_ap is not None else scr["qb"],
                                scr["qc"], scr["qbm"],
                                scr["qsc"], scr["qr"], scr["qsg"])
    nb = nelem // 16
    v_prev = r.cnt["vector"]
    r.wait("scalar", "vector", v_prev)
    sgc = r.op("scalar", lambda e: e.sign(out=sg[:, :nelem], in_=x_ap))
    r.op("vector", lambda e: e.reduce_max(out=qbm[:, :nb], in_=x_ap.rearrange("p (a b) -> p a b", b=16),
                                          axis=AX.X, apply_absolute_value=True))
    r.op("vector", lambda e: e.tensor_scalar(out=qsc[:, :nb], in0=qbm[:, :nb], scalar1=1.0 / 6.0,
                                             scalar2=1e-12, op0=ALU.mult, op1=ALU.max))
    vbm = r.op("vector", lambda e: e.reciprocal(out=qr[:, :nb], in_=qsc[:, :nb]))
    r.wait("scalar", "vector", vbm)
    abc = r.op("scalar", lambda e: e.activation(out=x_ap, in_=x_ap, func=AF.Abs))
    r.wait("vector", "scalar", abc)
    vnorm = r.op("vector", lambda e: e.tensor_tensor(out=x_ap.rearrange("p (a b) -> p a b", b=16),
                                                     in0=x_ap.rearrange("p (a b) -> p a b", b=16),
                                                     in1=qr[:, :nb].broadcast_to([128, nb, 16]), op=ALU.mult))
    r.op("vector", lambda e: e.tensor_scalar(out=qb[:, :nelem], in0=x_ap, scalar1=0.25,
                                             scalar2=None, op0=ALU.is_gt))
    for t in (0.75, 1.25, 1.75):
        r.op("vector", lambda e, t=t: e.scalar_tensor_tensor(out=qb[:, :nelem], in0=x_ap,
                                                             scalar=t, in1=qb[:, :nelem],
                                                             op0=ALU.is_gt, op1=ALU.add))
    r.op("vector", lambda e: e.tensor_scalar(out=qc[:, :nelem], in0=x_ap, scalar1=2.5,
                                             scalar2=None, op0=ALU.is_gt))
    for t in (3.5, 5.0, 5.0):
        r.op("vector", lambda e, t=t: e.scalar_tensor_tensor(out=qc[:, :nelem], in0=x_ap,
                                                             scalar=t, in1=qc[:, :nelem],
                                                             op0=ALU.is_gt, op1=ALU.add))
    r.op("vector", lambda e: e.scalar_tensor_tensor(out=qb[:, :nelem], in0=qb[:, :nelem], scalar=0.5,
                                                    in1=qc[:, :nelem], op0=ALU.mult, op1=ALU.add))
    r.op("vector", lambda e: e.tensor_tensor(out=qb[:, :nelem].rearrange("p (a b) -> p a b", b=16),
                                             in0=qb[:, :nelem].rearrange("p (a b) -> p a b", b=16),
                                             in1=qsc[:, :nb].broadcast_to([128, nb, 16]), op=ALU.mult))
    vc = r.op("vector", lambda e: e.tensor_tensor(out=out_ap, in0=qb[:, :nelem], in1=sg[:, :nelem],
                                                  op=ALU.mult))
    return vc


class PsumPool:
    def __init__(self, tensors):
        self.slots = [[t, None, 0] for t in tensors]
        self.i = 0

    def acquire(self, r, prod_eng):
        slot = self.slots[self.i % len(self.slots)]
        self.i += 1
        t, ceng, ccnt = slot
        if ceng is not None:
            r.wait(prod_eng, ceng, ccnt)
        return t, slot

    @staticmethod
    def release(slot, eng, cnt):
        slot[1], slot[2] = eng, cnt


def transpose_tiles(r, ident, pairs, pp, eng_evac="scalar"):
    for src, dst in pairs:
        t, slot = pp.acquire(r, "tensor")
        tb = t[:].bitcast(BF16)
        pcols = src.shape[0]
        rows = src.shape[1]
        tc = r.op("tensor", lambda e, tb=tb, src=src, pcols=pcols: e.transpose(tb[:, 0:pcols], src, ident[:]))
        r.wait(eng_evac, "tensor", tc)
        if eng_evac == "scalar":
            ec = r.op("scalar", lambda e, tb=tb, dst=dst, pcols=pcols, rows=rows: e.copy(out=dst, in_=tb[0:rows, 0:pcols]))
        else:
            ec = r.op("vector", lambda e, tb=tb, dst=dst, pcols=pcols, rows=rows: e.tensor_copy(out=dst, in_=tb[0:rows, 0:pcols]))
        PsumPool.release(slot, eng_evac, ec)


def quant_scratch(sb_, width=2816):
    return dict(
        sq=sb_("sq", [128, 1024], F32),
        st=sb_("st", [128, 8], F32),
        qb=sb_("qb", [128, width], BF16),
        qc=sb_("qc", [128, width], BF16),
        qbm=sb_("qbm", [128, width // 16], F32),
        qsc=sb_("qsc", [128, width // 16], F32),
        qr=sb_("qr", [128, width // 16], F32),
        qsg=sb_("qsg", [128, width], BF16),
    )


def emit_l0_ffn(nc, r, ctx, x1_d, wsl_d, wdsl_d, wqin_d, wqout_d, o_d):
    I = 2816
    IC = I // 128
    NBLK = [512, 512, 512, 512, 512, 256]
    if True:
        sb_ = lambda name, shape, dt=F32: ctx.enter_context(nc.sbuf_tensor(_un(name), shape, dt))[:]
        ident, ones_bf, ones_f, _ = build_ident(r, sb_)
        wnat = sb_("wnat", [128, 2816], F32)
        wgT = sb_("wgT", [128, 8 * I], BF16)
        wuT = sb_("wuT", [128, 8 * I], BF16)
        wdT = sb_("wdT", [128, IC * 1024], BF16)
        xqf = sb_("xqf", [128, 8 * TOK], BF16)
        h_f = sb_("h_f", [128, 1024], F32)
        hqf_t = sb_("hqf_t", [128, 2 * 512], BF16)
        scr = quant_scratch(sb_)
        xt = wnat[:, 0:1024]
        g_fs = [scr["sq"][:, 0:512], scr["sq"][:, 512:1024]]
        h2_fs = [wnat[:, 1536:2048], wnat[:, 2048:2560]]
        p_g = ctx.enter_context(nc.psum_tensor(_un("p_g"), [128, 512], F32))
        p_u = ctx.enter_context(nc.psum_tensor(_un("p_u"), [128, 512], F32))
        p_g2 = ctx.enter_context(nc.psum_tensor(_un("p_g2"), [128, 512], F32))
        p_u2 = ctx.enter_context(nc.psum_tensor(_un("p_u2"), [128, 512], F32))
        p_d0 = ctx.enter_context(nc.psum_tensor(_un("p_d0"), [128, 512], F32))
        p_d1 = ctx.enter_context(nc.psum_tensor(_un("p_d1"), [128, 512], F32))
        p_t0 = ctx.enter_context(nc.psum_tensor(_un("p_t0"), [128, 512], F32))
        p_t1 = ctx.enter_context(nc.psum_tensor(_un("p_t1"), [128, 512], F32))
        pp = PsumPool([p_t0, p_t1])

        wstg = xqf[:, 0:8960]
        hist = []
        for idx in range(7):
            if idx < 6:
                w = idx % 2
                wsl = wnat[:, w * 1024:(w + 1) * 1024]
                qsl = scr["qb"][:, w * 1024:(w + 1) * 1024]
                width = 1024
                src = wsl_d[idx * 128:(idx + 1) * 128, :]
                dcol = idx * 1024
                gi = idx - 2
            else:
                wsl = wnat[:, 0:2816]
                qsl = scr["qb"][:, 0:2816]
                width = 2816
                src = wdsl_d
                dcol = 6144
                gi = idx - 1
            guard_v = hist[gi][0] if gi >= 0 else 0
            guard_t = hist[gi][1] if gi >= 0 else 0
            r.wait("sync", "vector", guard_v)
            dn = r.dma("sync", wsl, src)
            r.dwait("vector", "sync", dn)
            r.dwait("scalar", "sync", dn)
            r.wait("vector", "tensor", guard_t)
            vq = emit_quant(r, wsl, qsl[:, 0:width], scr, width, qb_ap=qsl)
            r.wait("tensor", "vector", vq)
            pairs = [(qsl[:, j * 128:(j + 1) * 128], wstg[:, dcol + j * 128: dcol + (j + 1) * 128])
                     for j in range(width // 128)]
            transpose_tiles(r, ident, pairs, pp)
            hist.append((vq, r.cnt["tensor"]))
        r.wait("sync", "vector", hist[-1][0])
        r.wait("sync", "scalar", r.cnt["scalar"])
        dwc = r.dma("sync", wqin_d, wstg[:])
        r.dwait("gpsimd", "sync", dwc)
        ncc = r.cc(lambda e: e.collective_compute(
            "AllGather", ALU.bypass, replica_groups=[list(range(8))],
            ins=[wqin_d.opt()], outs=[wqout_d.opt()]))
        r.cwait("gpsimd", ncc)
        for g in range(44):
            rk, s_ = divmod(g, 6)
            dstT, i_ = (wgT, g) if g < 22 else (wuT, g - 22)
            for kc in range(8):
                r.dma("gpsimd", dstT[:, kc * I + i_ * 128: kc * I + (i_ + 1) * 128],
                      wqout_d[rk * 128:(rk + 1) * 128, s_ * 1024 + kc * 128: s_ * 1024 + (kc + 1) * 128])
        for rk in range(8):
            for j in range(IC):
                r.dma("gpsimd", wdT[:, j * 1024 + rk * 128: j * 1024 + (rk + 1) * 128],
                      wqout_d[rk * 128:(rk + 1) * 128, 6144 + j * 128: 6144 + (j + 1) * 128])
        reasm = r.dcnt["gpsimd"]
        r.dwait("scalar", "sync", dwc)
        r.dwait("vector", "sync", dwc)

        for tt in range(16):
            t_prev = r.cnt["tensor"]
            dn = r.dma("sync", xt[:], x1_d[tt * 128:(tt + 1) * 128, :])
            r.dwait("vector", "sync", dn)
            r.dwait("scalar", "sync", dn)
            emit_ln(r, xt[:], h_f[:], scr)
            r.wait("vector", "tensor", t_prev)
            vq = emit_quant(r, h_f[:], scr["qb"][:, 0:1024], scr, 1024)
            r.wait("tensor", "vector", vq)
            pairs = [(scr["qb"][:, j * 128:(j + 1) * 128],
                      xqf[:, j * TOK + tt * 128: j * TOK + (tt + 1) * 128]) for j in range(8)]
            transpose_tiles(r, ident, pairs, pp)
            r.wait("sync", "scalar", r.cnt["scalar"])

        r.wait("tensor", "scalar", r.cnt["scalar"])
        r.dwait("tensor", "gpsimd", reasm)
        dcons = [None, None]
        prev_store = 0
        silu_cnt = [0, 0]
        h2_cnt = [0, 0]
        tc_cnt = [0, 0]
        down_cnt = [0, 0]
        pend_down = [None]
        p_gs, p_us = [p_g, p_g2], [p_u, p_u2]
        starts = [0, 512, 1024, 1536, 2048, 2560]
        blocks = [(tt, bi, starts[bi]) for tt in range(16) for bi in range(6)]
        gate_cnt = [0, 0]
        up_cnt = [0, 0]

        def emit_gateup(idx):
            tt, bi, n0 = blocks[idx]
            nb = NBLK[bi]
            w = idx % 2
            pg, pu = p_gs[w], p_us[w]
            r.wait("tensor", "scalar", silu_cnt[w])
            for kc in range(8):
                lh = xqf[:, kc * TOK + tt * 128: kc * TOK + (tt + 1) * 128]
                gate_cnt[w] = r.op("tensor", lambda e, lh=lh, kc=kc, n0=n0, nb=nb, pg=pg:
                                   e.matmul(pg[:, 0:nb], lh, wgT[:, kc * I + n0: kc * I + n0 + nb],
                                            start=(kc == 0), stop=(kc == 7)))
            r.wait("tensor", "vector", h2_cnt[w])
            for kc in range(8):
                lh = xqf[:, kc * TOK + tt * 128: kc * TOK + (tt + 1) * 128]
                up_cnt[w] = r.op("tensor", lambda e, lh=lh, kc=kc, n0=n0, nb=nb, pu=pu:
                                 e.matmul(pu[:, 0:nb], lh, wuT[:, kc * I + n0: kc * I + n0 + nb],
                                          start=(kc == 0), stop=(kc == 7)))

        def emit_silu(idx):
            w = idx % 2
            nb = NBLK[blocks[idx][1]]
            r.wait("scalar", "tensor", gate_cnt[w])
            r.wait("scalar", "vector", h2_cnt[w])
            sl = r.op("scalar", lambda e, nb=nb, pg=p_gs[w], g_f=g_fs[w]: e.activation(
                out=g_f[:, 0:nb], in_=pg[:, 0:nb], func=AF.Silu))
            silu_cnt[w] = sl

        emit_gateup(0)
        emit_silu(0)
        for idx in range(96):
            tt, bi, n0 = blocks[idx]
            nb = NBLK[bi]
            w = idx % 2
            pg, pu = p_gs[w], p_us[w]
            g_f, h2_f = g_fs[w], h2_fs[w]
            qsl = scr["qb"][:, w * 1024:w * 1024 + 512]
            umm = up_cnt[w]
            sl = silu_cnt[w]
            if idx + 1 < 96:
                emit_gateup(idx + 1)
            if pend_down[0] is not None:
                pend_down[0]()
                pend_down[0] = None
            r.wait("vector", "tensor", umm)
            r.wait("vector", "scalar", sl)
            h2 = r.op("vector", lambda e, nb=nb, pu=pu, g_f=g_f, h2_f=h2_f: e.tensor_tensor(
                out=h2_f[:, 0:nb], in0=pu[:, 0:nb], in1=g_f[:, 0:nb], op=ALU.mult))
            h2_cnt[w] = h2
            r.wait("vector", "tensor", tc_cnt[w])
            vq = emit_quant(r, h2_f[:, 0:nb], qsl[:, 0:nb], scr, nb, qb_ap=qsl)
            if idx + 1 < 96:
                emit_silu(idx + 1)
            r.wait("scalar", "tensor", down_cnt[w])
            r.wait("tensor", "vector", vq)
            pairs = [(qsl[:, j * 128:(j + 1) * 128], hqf_t[:, w * 512 + j * 128: w * 512 + (j + 1) * 128])
                     for j in range(nb // 128)]
            transpose_tiles(r, ident, pairs, pp)
            tc_cnt[w] = r.cnt["tensor"]
            tr_done = r.cnt["scalar"]

            def mk_down(bi=bi, nb=nb, n0=n0, w=w, tr_done=tr_done):
                def emit():
                    r.wait("tensor", "scalar", tr_done)
                    if bi == 0:
                        for cn in range(2):
                            r.wait("tensor", "vector", dcons[cn])
                    for cn, p_dn in enumerate((p_d0, p_d1)):
                        for j in range(nb // 128):
                            ic = (n0 // 128) + j
                            first = (bi == 0 and j == 0)
                            lastmm = (bi == len(NBLK) - 1 and j == (nb // 128) - 1)
                            r.op("tensor", lambda e, j=j, ic=ic, p_dn=p_dn, cn=cn, first=first,
                                 lastmm=lastmm, w=w:
                                 e.matmul(p_dn[:, :], hqf_t[:, w * 512 + j * 128: w * 512 + (j + 1) * 128],
                                          wdT[:, ic * 1024 + cn * 512: ic * 1024 + (cn + 1) * 512],
                                          start=first, stop=lastmm))
                    down_cnt[w] = r.cnt["tensor"]
                return emit
            pend_down[0] = mk_down()
            if bi != 5:
                continue
            pend_down[0]()
            pend_down[0] = None
            down_done = r.cnt["tensor"]
            dn = r.dma("sync", xt[:], x1_d[tt * 128:(tt + 1) * 128, :])
            r.dwait("vector", "sync", dn)
            r.dwait("vector", "sync", prev_store)
            r.wait("vector", "tensor", down_done)
            v0 = r.op("vector", lambda e: e.tensor_tensor(out=h_f[:, 0:512], in0=xt[:, 0:512],
                                                          in1=p_d0[:, :], op=ALU.add))
            v1 = r.op("vector", lambda e: e.tensor_tensor(out=h_f[:, 512:1024], in0=xt[:, 512:1024],
                                                          in1=p_d1[:, :], op=ALU.add))
            dcons = [v0, v1]
            r.wait("sync", "vector", v1)
            prev_store = r.dma("sync", o_d[tt * 128:(tt + 1) * 128, :], h_f[:])
        r.dwait("sync", "sync", r.dcnt["sync"])


def build_l0_ffn():
    nc = bass.Bass("TRN2", target_bir_lowering=False)
    x1_d = nc.dram_tensor("x1", [TOK, D], F32, kind="ExternalInput").ap()
    wg_d = nc.dram_tensor("wg", [2816, D], F32, kind="ExternalInput").ap()
    wu_d = nc.dram_tensor("wu", [2816, D], F32, kind="ExternalInput").ap()
    wd_d = nc.dram_tensor("wd", [D, 2816], F32, kind="ExternalInput").ap()
    o_d = nc.dram_tensor("o_x2", [TOK, D], F32, kind="ExternalOutput").ap()
    with ExitStack() as ctx:
        r = Rec(nc, ctx)
        with ExitStack() as s:
            emit_l0_ffn(nc, r, s, x1_d, wg_d, wu_d, wd_d, o_d)
        blk = ctx.enter_context(nc.Block())
        r.finalize(blk)
    return nc


CMW = 10112


def emit_l0_attn(nc, r, ctx, xb_d, xo_d, cm_d, wqT_d, wkvd_d, wkupT_d, wvupT_d, wo_d, o_d):
    SCL = 1.0 / np.sqrt(128.0)
    if True:
        sb_ = lambda name, shape, dt=F32: ctx.enter_context(nc.sbuf_tensor(_un(name), shape, dt))[:]
        ident, ones_bf, ones_f, _ = build_ident(r, sb_)
        hown = sb_("hown", [128, 8 * TOK], BF16)
        attn_f = hown
        wqT_b = hown[:, 0:8192]
        wo_b = hown[:, 8192:16384]
        wkvd_b = sb_("wkvd_b", [128, 8 * 128], BF16)
        wkupT_b = sb_("wkupT_b", [128, 32], BF16)
        wvupT_b = sb_("wvupT_b", [128, 32], BF16)
        wq_abs = sb_("wq_abs", [128, 8 * 256], BF16)
        wo_abs = sb_("wo_abs", [128, 2 * 1024], BF16)
        k_lat = sb_("k_lat", [128, 2 * SEQ], BF16)
        v_aug = sb_("v_aug", [128, 64 * 2 * 33], BF16)
        h_fmb = sb_("h_fmb", [128, 8 * 1024], BF16)
        qabs = sb_("qabs", [128, 3 * TOK], BF16)
        ylat = sb_("ylat", [128, 2 * TOK], BF16)
        cmask = sb_("cmask", [128, CMW], BF16)
        probs = sb_("probs", [128, 4 * 512], BF16)
        probs2 = sb_("probs2", [128, 4 * 512], BF16)
        h_bf = sb_("h_bf", [128, 1024], BF16)
        kv_st = sb_("kv_st", [128, 1024], BF16)
        sqs = hown[:, 0:6144]
        sqk = hown[:, 6144:14336]
        stat = sb_("stat", [128, 64], F32)
        bias_q = sb_("bias_q", [128, 32], F32)
        recb = sb_("recb", [128, 4 * 512], F32)
        ysb = sb_("ysb", [128, 4 * 512], F32)
        ynst = sb_("ynst", [128, 512], BF16)
        xt = sb_("xt", [128, 1024], F32)
        res = sb_("res", [128, 1024], F32)
        p_s = [ctx.enter_context(nc.psum_tensor(f"p_s{i}", [128, 512], F32)) for i in range(4)]
        p_y = [ctx.enter_context(nc.psum_tensor(f"p_y{i}", [128, 512], F32)) for i in range(4)]
        pp = PsumPool(p_s)
        ppsc = pp
        dcm = r.dma("sync", cmask[:], cm_d)
        r.dwait("vector", "sync", dcm)

        for i in range(8):
            r.dma("gpsimd", wqT_b[:, i * 1024:(i + 1) * 1024], wqT_d[i * 128:(i + 1) * 128, :])
            r.dma("gpsimd", wo_b[:, i * 1024:(i + 1) * 1024], wo_d[i * 128:(i + 1) * 128, :])
        for i in range(8):
            r.dma("gpsimd", wkvd_b[:, i * 128:(i + 1) * 128], wkvd_d[i * 128:(i + 1) * 128, :])
        r.dma("gpsimd", wkupT_b[:], wkupT_d)
        dw = r.dma("gpsimd", wvupT_b[:], wvupT_d)
        r.dwait("tensor", "gpsimd", dw)

        for hr in range(8):
            for cc in range(8):
                t, slot = ppsc.acquire(r, "tensor")
                mm = r.op("tensor", lambda e, hr=hr, cc=cc, t=t:
                          e.matmul(t[:, 0:32], wqT_b[:, hr * 1024 + cc * 128: hr * 1024 + (cc + 1) * 128],
                                   wkupT_b[:], start=True, stop=True))
                r.wait("scalar", "tensor", mm)
                ec = r.op("scalar", lambda e, hr=hr, cc=cc, t=t:
                          e.copy(out=wq_abs[:, cc * 256 + hr * 32: cc * 256 + hr * 32 + 32], in_=t[:, 0:32]))
                PsumPool.release(slot, "scalar", ec)
        for hr in range(8):
            for nb in range(2):
                t, slot = ppsc.acquire(r, "tensor")
                mm = r.op("tensor", lambda e, hr=hr, nb=nb, t=t:
                          e.matmul(t[0:32, :], wvupT_b[:], wo_b[:, hr * 1024 + nb * 512: hr * 1024 + (nb + 1) * 512],
                                   start=True, stop=True))
                r.wait("scalar", "tensor", mm)
                ec = r.op("scalar", lambda e, t=t: e.copy(out=kv_st[0:32, 0:512], in_=t[0:32, :]))
                r.wait("gpsimd", "scalar", ec)
                dmn = r.dma("gpsimd", wo_abs[32 * (hr % 4):32 * (hr % 4) + 32,
                                             (hr // 4) * 1024 + nb * 512:(hr // 4) * 1024 + (nb + 1) * 512],
                            kv_st[0:32, 0:512])
                r.dwait("scalar", "gpsimd", dmn)
                PsumPool.release(slot, "scalar", ec)

        xt2 = sb_("xt2", [128, 2 * 1024], F32)
        hb2 = sb_("hb2", [128, 2 * 1024], BF16)
        scr2 = [dict(sq=sb_("sq2a", [128, 1024], F32), st=stat[:, 8:16]),
                dict(sq=sb_("sq2b", [128, 1024], F32), st=stat[:, 16:24])]
        lnhist = []

        def ln_tile(src_ap, dsts):
            i = len(lnhist)
            w = i % 2
            gv, gt = lnhist[i - 2] if i >= 2 else (0, 0)
            r.wait("sync", "vector", gv)
            dn = r.dma("sync", xt2[:, w * 1024:(w + 1) * 1024], src_ap)
            r.dwait("vector", "sync", dn)
            r.dwait("scalar", "sync", dn)
            r.wait("vector", "tensor", gt)
            vc = emit_ln(r, xt2[:, w * 1024:(w + 1) * 1024], hb2[:, w * 1024:(w + 1) * 1024],
                         scr2[w], swait=gv)
            r.wait("tensor", "vector", vc)
            pairs = [(hb2[:, w * 1024 + j * 128: w * 1024 + (j + 1) * 128], dsts[j]) for j in range(8)]
            transpose_tiles(r, ident, pairs, pp)
            lnhist.append((vc, r.cnt["tensor"]))

        vcm = r.op("vector", lambda e: e.memset(v_aug[:], 1.0))
        r.wait("tensor", "vector", vcm)
        for kb in range(8):
            for tsub in range(8):
                tt_g = kb * 8 + tsub
                ln_tile(xb_d[tt_g * 128:(tt_g + 1) * 128, :],
                        [h_fmb[:, j * 1024 + tsub * 128: j * 1024 + (tsub + 1) * 128] for j in range(8)])
            r.wait("tensor", "scalar", r.cnt["scalar"])
            for nb in range(2):
                t, slot = ppsc.acquire(r, "tensor")
                mm = None
                for kc in range(8):
                    mm = r.op("tensor", lambda e, kc=kc, nb=nb, t=t:
                              e.matmul(t[:, :], wkvd_b[:, kc * 128:(kc + 1) * 128],
                                       h_fmb[:, kc * 1024 + nb * 512: kc * 1024 + (nb + 1) * 512],
                                       start=(kc == 0), stop=(kc == 7)))
                r.wait("scalar", "tensor", mm)
                ec = r.op("scalar", lambda e, t=t: e.copy(out=kv_st[:, 0:512], in_=t[:, :]))
                r.wait("gpsimd", "scalar", ec)
                dl = None
                for h in range(2):
                    for rep in range(3):
                        dl = r.dma("gpsimd", k_lat[32 * rep:32 * rep + 32,
                                                   h * SEQ + kb * 1024 + nb * 512: h * SEQ + kb * 1024 + (nb + 1) * 512],
                                   kv_st[32 * h:32 * h + 32, 0:512])
                r.dwait("tensor", "gpsimd", dl)
                r.dwait("scalar", "gpsimd", dl)
                for j in range(4):
                    kt_g = kb * 8 + nb * 4 + j
                    t2, slot2 = pp.acquire(r, "tensor")
                    t2b = t2[:].bitcast(BF16)
                    tc = r.op("tensor", lambda e, t2b=t2b, j=j: e.transpose(
                        t2b[:, 0:64], kv_st[64:128, j * 128:(j + 1) * 128], ident[64:128, 64:128]))
                    r.wait("vector", "tensor", tc)
                    v1 = r.op("vector", lambda e, t2b=t2b, kt_g=kt_g: e.tensor_copy(
                        out=v_aug[:, (kt_g * 2 + 0) * 33:(kt_g * 2 + 0) * 33 + 32], in_=t2b[:, 0:32]))
                    v2 = r.op("vector", lambda e, t2b=t2b, kt_g=kt_g: e.tensor_copy(
                        out=v_aug[:, (kt_g * 2 + 1) * 33:(kt_g * 2 + 1) * 33 + 32], in_=t2b[:, 32:64]))
                    PsumPool.release(slot2, "vector", v2)
                PsumPool.release(slot, "vector", r.cnt["vector"])

        for tt in range(16):
            ln_tile(xo_d[tt * 128:(tt + 1) * 128, :],
                    [hown[:, j * TOK + tt * 128: j * TOK + (tt + 1) * 128] for j in range(8)])
        r.wait("tensor", "scalar", r.cnt["scalar"])
        for ch in range(3):
            mwid = 96 if ch < 2 else 64
            for nb in range(4):
                t, slot = ppsc.acquire(r, "tensor")
                mm = None
                for cc in range(8):
                    mm = r.op("tensor", lambda e, ch=ch, mwid=mwid, nb=nb, cc=cc, t=t:
                              e.matmul(t[0:mwid, :], wq_abs[:, cc * 256 + ch * 96: cc * 256 + ch * 96 + mwid],
                                       hown[:, cc * TOK + nb * 512: cc * TOK + (nb + 1) * 512],
                                       start=(cc == 0), stop=(cc == 7)))
                r.wait("scalar", "tensor", mm)
                ec = r.op("scalar", lambda e, ch=ch, mwid=mwid, nb=nb, t=t: e.copy(
                    out=qabs[0:mwid, ch * TOK + nb * 512: ch * TOK + (nb + 1) * 512], in_=t[0:mwid, :]))
                PsumPool.release(slot, "scalar", ec)

        sq1 = r.op("scalar", lambda e: e.activation(out=sqs[:], in_=qabs[:], func=AF.Square))
        for h in range(2):
            r.wait("scalar", "tensor", r.cnt["tensor"])
            sq2 = r.op("scalar", lambda e, h=h: e.activation(out=sqk[0:32, 0:SEQ],
                                                             in_=k_lat[0:32, h * SEQ:(h + 1) * SEQ],
                                                             func=AF.Square))
            r.wait("tensor", "scalar", sq2)
            for i in range(16):
                t, slot = ppsc.acquire(r, "tensor")
                mm = r.op("tensor", lambda e, h=h, i=i, t=t: e.matmul(
                    t[0:1, :], ones_bf[0:32, 0:1],
                    sqk[0:32, i * 512:(i + 1) * 512], start=True, stop=True))
                r.wait("vector", "tensor", mm)
                v1 = r.op("vector", lambda e, t=t: e.reduce_max(out=stat[0:1, 40:41], in_=t[0:1, :], axis=AX.X))
                if i == 0:
                    v2 = r.op("vector", lambda e, h=h: e.tensor_copy(out=stat[0:1, 32 + h:33 + h],
                                                                     in_=stat[0:1, 40:41]))
                else:
                    v2 = r.op("vector", lambda e, h=h: e.tensor_tensor(out=stat[0:1, 32 + h:33 + h],
                                                                       in0=stat[0:1, 32 + h:33 + h],
                                                                       in1=stat[0:1, 40:41], op=ALU.max))
                PsumPool.release(slot, "vector", v2)
        for hr in range(8):
            for qb in range(4):
                t, slot = ppsc.acquire(r, "tensor")
                b0 = 32 * (hr % 3)
                mm = r.op("tensor", lambda e, hr=hr, qb=qb, t=t, b0=b0: e.matmul(
                    t[0:1, :], ones_bf[b0:b0 + 32, 0:1],
                    sqs[b0:b0 + 32, (hr // 3) * TOK + qb * 512:(hr // 3) * TOK + (qb + 1) * 512],
                    start=True, stop=True))
                r.wait("vector", "tensor", mm)
                r.op("vector", lambda e, t=t: e.reduce_max(out=stat[0:1, 41:42], in_=t[0:1, :], axis=AX.X))
                r.op("vector", lambda e, hr=hr: e.tensor_tensor(out=stat[0:1, 42:43], in0=stat[0:1, 41:42],
                                                                in1=stat[0:1, 32 + hr // 4:33 + hr // 4],
                                                                op=ALU.mult))
                vs = r.op("vector", lambda e: e.tensor_copy(out=stat[0:1, 43:44], in_=stat[0:1, 42:43]))
                r.wait("scalar", "vector", vs)
                ss = r.op("scalar", lambda e: e.activation(out=stat[0:1, 44:45], in_=stat[0:1, 43:44],
                                                           func=AF.Sqrt))
                r.wait("vector", "scalar", ss)
                vb = r.op("vector", lambda e: e.tensor_scalar(out=stat[0:1, 45:46], in0=stat[0:1, 44:45],
                                                              scalar1=-SCL * 1.02, scalar2=-1.0,
                                                              op0=ALU.mult, op1=ALU.add))
                r.wait("tensor", "vector", vb)
                mm2 = r.op("tensor", lambda e, t=t: e.matmul(t[:, 0:1], ones_f[0:1, :], stat[0:1, 45:46],
                                                             start=True, stop=True))
                r.wait("vector", "tensor", mm2)
                ve = r.op("vector", lambda e, hr=hr, qb=qb, t=t: e.tensor_copy(
                    out=bias_q[:, hr * 4 + qb: hr * 4 + qb + 1], in_=t[:, 0:1]))
                PsumPool.release(slot, "vector", ve)

        yslot_cons = [None] * 4
        it = [0]
        pend = []
        pool_cons = [0, 0, 0, 0]
        probs_w = [0, 0, 0, 0]
        prev_dmy = None
        for qb in range(4):
            for half in range(2):
                for kb in range(8):
                    for hri in range(4):
                        hr = half * 4 + hri
                        h = hr // 4
                        b0 = 32 * (hr % 3)
                        for kt in range(8):
                            kt_g = kb * 8 + kt
                            i = it[0]
                            it[0] += 1
                            pi = i % 4
                            t, slot = ppsc.acquire(r, "tensor")
                            mm = r.op("tensor", lambda e, t=t, h=h, b0=b0, hr=hr, qb=qb, kt_g=kt_g: e.matmul(
                                t[:, :], k_lat[b0:b0 + 32, h * SEQ + kt_g * 128: h * SEQ + (kt_g + 1) * 128],
                                qabs[b0:b0 + 32, (hr // 3) * TOK + qb * 512:(hr // 3) * TOK + (qb + 1) * 512],
                                start=True, stop=True))
                            while len(pend) >= 4:
                                pend.pop(0)()
                            off = qb * 512 - kt_g * 128 + 8064
                            r.wait("scalar", "tensor", mm)
                            r.wait("scalar", "vector", pool_cons[pi])
                            pe = r.op("scalar", lambda e, hr=hr, qb=qb, t=t, pi=pi: e.activation(
                                out=probs[:, pi * 512:(pi + 1) * 512], in_=t[:, :],
                                func=AF.Exp, scale=SCL,
                                bias=bias_q[:, hr * 4 + qb: hr * 4 + qb + 1]))
                            PsumPool.release(slot, "scalar", pe)
                            r.wait("vector", "scalar", pe)
                            r.wait("vector", "tensor", probs_w[pi])
                            pmu = r.op("vector", lambda e, pi=pi, off=off: e.tensor_tensor(
                                out=probs2[:, pi * 512:(pi + 1) * 512],
                                in0=probs[:, pi * 512:(pi + 1) * 512],
                                in1=cmask[:, off:off + 512], op=ALU.mult))
                            pool_cons[pi] = pmu

                            def mk_pv(hri=hri, h=h, kt_g=kt_g, kb=kb, kt=kt, pi=pi, pmu=pmu):
                                first = (kb == 0 and kt == 0)
                                lastmm = (kb == 7 and kt == 7)

                                def emit():
                                    r.wait("tensor", "vector", pmu)
                                    if first:
                                        r.wait("tensor", "vector", yslot_cons[hri])
                                    c = r.op("tensor", lambda e: e.matmul(
                                        p_y[hri][0:33, :], v_aug[:, (kt_g * 2 + h) * 33:(kt_g * 2 + h) * 33 + 33],
                                        probs2[:, pi * 512:(pi + 1) * 512], start=first, stop=lastmm))
                                    probs_w[pi] = c
                                return emit
                            pend.append(mk_pv())
                while pend:
                    pend.pop(0)()
                last_pv = r.cnt["tensor"]
                for hri in range(4):
                    r.wait("vector", "tensor", last_pv)
                    ev = r.op("vector", lambda e, hri=hri: e.tensor_copy(
                        out=ysb[0:33, hri * 512:(hri + 1) * 512], in_=p_y[hri][0:33, :]))
                    yslot_cons[hri] = ev
                for hri in range(4):
                    hr = half * 4 + hri
                    rb = recb[:, hri * 512:(hri + 1) * 512]
                    v1 = r.op("vector", lambda e, hri=hri, rb=rb: e.reciprocal(
                        out=rb[32:33, :], in_=ysb[32:33, hri * 512:(hri + 1) * 512]))
                    r.wait("sync", "vector", v1)
                    dprev = r.dma("sync", rb[0:1, 0:512], rb[32:33, 0:512])
                    for lo, n in ((1, 1), (2, 2), (4, 4), (8, 8), (16, 16)):
                        r.dwait("sync", "sync", dprev)
                        dprev = r.dma("sync", rb[lo:lo + n, 0:512], rb[0:n, 0:512])
                    r.dwait("vector", "sync", dprev)
                    r.dwait("vector", "sync", prev_dmy)
                    vn = r.op("vector", lambda e, hri=hri, rb=rb: e.tensor_tensor(
                        out=ynst[0:32, :], in0=ysb[0:32, hri * 512:(hri + 1) * 512],
                        in1=rb[0:32, :], op=ALU.mult))
                    r.wait("sync", "vector", vn)
                    prev_dmy = r.dma("sync", ylat[32 * (hr % 4):32 * (hr % 4) + 32,
                                                  (hr // 4) * TOK + qb * 512:(hr // 4) * TOK + (qb + 1) * 512],
                                     ynst[0:32, 0:512])
        dly = r.dcnt["sync"]
        r.dwait("tensor", "sync", dly)
        for cc in range(8):
            for nb in range(4):
                t, slot = ppsc.acquire(r, "tensor")
                mm = None
                for m in range(2):
                    mm = r.op("tensor", lambda e, cc=cc, nb=nb, m=m, t=t: e.matmul(
                        t[:, :], wo_abs[:, m * 1024 + cc * 128: m * 1024 + (cc + 1) * 128],
                        ylat[:, m * TOK + nb * 512: m * TOK + (nb + 1) * 512],
                        start=(m == 0), stop=(m == 1)))
                r.wait("scalar", "tensor", mm)
                ec = r.op("scalar", lambda e, cc=cc, nb=nb, t=t: e.copy(
                    out=attn_f[:, cc * TOK + nb * 512: cc * TOK + (nb + 1) * 512], in_=t[:, :]))
                PsumPool.release(slot, "scalar", ec)
        sdone = r.cnt["scalar"]
        r.wait("tensor", "scalar", sdone)
        prev_store = 0
        for tt in range(16):
            dn = r.dma("sync", xt[:], xo_d[tt * 128:(tt + 1) * 128, :])
            pairs = [(attn_f[:, j * TOK + tt * 128: j * TOK + (tt + 1) * 128],
                      h_bf[:, j * 128:(j + 1) * 128]) for j in range(8)]
            transpose_tiles(r, ident, pairs, pp, eng_evac="vector")
            r.dwait("vector", "sync", dn)
            r.dwait("vector", "sync", prev_store)
            va = r.op("vector", lambda e: e.tensor_tensor(out=res[:], in0=xt[:], in1=h_bf[:], op=ALU.add))
            r.wait("sync", "vector", va)
            prev_store = r.dma("sync", o_d[tt * 128:(tt + 1) * 128, :], res[:])
        r.dwait("sync", "sync", r.dcnt["sync"])


def build_l0_attn():
    nc = bass.Bass("TRN2", target_bir_lowering=False)
    xb_d = nc.dram_tensor("xb", [SEQ, D], F32, kind="ExternalInput").ap()
    xo_d = nc.dram_tensor("xo", [TOK, D], F32, kind="ExternalInput").ap()
    mk_d = nc.dram_tensor("cmask", [128, CMW], BF16, kind="ExternalInput").ap()
    wqT_d = nc.dram_tensor("wqT", [D, D], F32, kind="ExternalInput").ap()
    wkvd_d = nc.dram_tensor("wkvd", [D, 128], F32, kind="ExternalInput").ap()
    wkupT_d = nc.dram_tensor("wkupT", [128, 32], F32, kind="ExternalInput").ap()
    wvupT_d = nc.dram_tensor("wvupT", [128, 32], F32, kind="ExternalInput").ap()
    wo_d = nc.dram_tensor("wo", [D, D], F32, kind="ExternalInput").ap()
    o_d = nc.dram_tensor("o_x1", [TOK, D], F32, kind="ExternalOutput").ap()
    with ExitStack() as ctx:
        r = Rec(nc, ctx)
        with ExitStack() as s:
            emit_l0_attn(nc, r, s, xb_d, xo_d, mk_d, wqT_d, wkvd_d, wkupT_d, wvupT_d, wo_d, o_d)
        blk = ctx.enter_context(nc.Block())
        r.finalize(blk)
    return nc


def emit_l1_attn(nc, r, ctx, x2o_d, sel_d, ccin_d, ccout_d, wvT_d, wfcT_d, bfc_d, gam_d, bet_d,
                 scr_dram, o_writes):
    if True:
        sb_ = lambda name, shape, dt=F32: ctx.enter_context(nc.sbuf_tensor(_un(name), shape, dt))[:]
        ident, ones_bf, ones_f, _ = build_ident(r, sb_)
        wv_f = sb_("wv_f", [128, 8 * 1024], F32)
        wfc_f = sb_("wfc_f", [128, 8 * 1024], F32)
        ccsb = sb_("ccsb", [16, 1024], F32)
        selsb = sb_("selsb", [16, 2], F32)
        x2l = sb_("x2l", [1, 1024], F32)
        gam_r = sb_("gam_r", [128, 1024], F32)
        bet_r = sb_("bet_r", [128, 1024], F32)
        bfc_c = sb_("bfc_c", [128, 8], F32)
        h_bf = sb_("h_bf", [128, 1024], BF16)
        xt = sb_("xt", [128, 1024], F32)
        u_f = sb_("u_f", [128, 1024], F32)
        h2_f = sb_("h2_f", [128, 1024], F32)
        res = sb_("res", [128, 1024], F32)
        hrow = sb_("hrow", [128, 4096], F32)
        hcol = sb_("hcol", [128, 8], F32)
        vcol = sb_("vcol", [128, 8], F32)
        o2col = sb_("o2col", [128, 8], F32)
        o2rep = sb_("o2rep", [128, 1024], F32)
        stat = sb_("stat", [128, 24], F32)
        scr = dict(sq=sb_("sq", [128, 1024], F32), st=stat[:, 0:8])
        p_a = ctx.enter_context(nc.psum_tensor(_un("p_a"), [128, 512], F32))
        p_b = ctx.enter_context(nc.psum_tensor(_un("p_b"), [128, 512], F32))
        p_c = ctx.enter_context(nc.psum_tensor(_un("p_c"), [128, 512], F32))

        for i in range(8):
            r.dma("sync", wv_f[:, i * 1024:(i + 1) * 1024], wvT_d[i * 128:(i + 1) * 128, :])
            r.dma("sync", wfc_f[:, i * 1024:(i + 1) * 1024], wfcT_d[i * 128:(i + 1) * 128, :])
        dv = r.dma("sync", hrow[0:1, 1024:2048], gam_d[None, :])
        dv = r.dma("sync", hrow[0:1, 2048:3072], bet_d[None, :])
        dvb = r.dma("sync", bfc_c[:], bfc_d.rearrange("(c p) -> p c", p=128), allow_slow_non_contiguous=True)
        r.dma("sync", selsb[:], sel_d)
        r.dwait("tensor", "sync", dvb)
        for (row, dst) in ((1, gam_r), (2, bet_r)):
            for nb in range(2):
                mm = r.op("tensor", lambda e, row=row, nb=nb: e.matmul(
                    (p_a if nb == 0 else p_b)[:, :], ones_f[0:1, :],
                    hrow[0:1, row * 1024 + nb * 512: row * 1024 + (nb + 1) * 512], start=True, stop=True))
                r.wait("vector", "tensor", mm)
                ve = r.op("vector", lambda e, dst=dst, nb=nb: e.tensor_copy(
                    out=dst[:, nb * 512:(nb + 1) * 512], in_=(p_a if nb == 0 else p_b)[:, :]))
        vinit = r.cnt["vector"]
        r.wait("tensor", "vector", vinit)

        xt2l = sb_("xt2l", [128, 2 * 1024], F32)
        hb2l = sb_("hb2l", [128, 2 * 1024], BF16)
        scr2l = [dict(sq=sb_("sq2c", [128, 1024], F32), st=stat[:, 8:16]),
                 dict(sq=sb_("sq2d", [128, 1024], F32), st=stat[:, 16:24])]
        cs_hist = []
        for tt in range(16):
            w = tt % 2
            gv, gt = cs_hist[tt - 2] if tt >= 2 else (0, 0)
            r.wait("sync", "vector", gv)
            dn = r.dma("sync", xt2l[:, w * 1024:(w + 1) * 1024], x2o_d[tt * 128:(tt + 1) * 128, :])
            r.dwait("vector", "sync", dn)
            r.dwait("scalar", "sync", dn)
            r.wait("vector", "tensor", gt)
            vc = emit_ln(r, xt2l[:, w * 1024:(w + 1) * 1024], hb2l[:, w * 1024:(w + 1) * 1024],
                         scr2l[w], swait=gv)
            r.wait("tensor", "vector", vc)
            for nb in range(2):
                r.op("tensor", lambda e, nb=nb, tt=tt, w=w: e.matmul(
                    (p_a if nb == 0 else p_b)[0:1, :], ones_bf[:],
                    hb2l[:, w * 1024 + nb * 512: w * 1024 + (nb + 1) * 512],
                    start=(tt == 0), stop=(tt == 15)))
            cs_hist.append((vc, r.cnt["tensor"]))
        mmc = r.cnt["tensor"]
        r.wait("vector", "tensor", mmc)
        r.op("vector", lambda e: e.tensor_copy(out=hrow[0:1, 0:512], in_=p_a[0:1, :]))
        vh = r.op("vector", lambda e: e.tensor_copy(out=hrow[0:1, 512:1024], in_=p_b[0:1, :]))
        r.wait("sync", "vector", vh)
        d1 = r.dma("sync", x2l[0:1, :], x2o_d[2047:2048, :])
        r.dma("sync", ccin_d[1:2, :], hrow[0:1, 0:1024])
        r.dwait("sync", "sync", d1)
        dci = r.dma("sync", ccin_d[0:1, :], x2l[0:1, :])
        r.dwait("gpsimd", "sync", dci)
        ncc = r.cc(lambda e: e.collective_compute(
            "AllGather", ALU.bypass, replica_groups=[list(range(8))],
            ins=[ccin_d.opt()], outs=[ccout_d.opt()]))
        r.cwait("sync", ncc)
        dcg = r.dma("sync", ccsb[:], ccout_d)
        r.dwait("tensor", "sync", dcg)
        for nb in range(2):
            mm = r.op("tensor", lambda e, nb=nb: e.matmul(
                p_a[0:1, :], selsb[:, 0:1], ccsb[:, nb * 512:(nb + 1) * 512],
                start=True, stop=True))
            mm2 = r.op("tensor", lambda e, nb=nb: e.matmul(
                p_b[0:1, :], selsb[:, 1:2], ccsb[:, nb * 512:(nb + 1) * 512],
                start=True, stop=True))
            r.wait("vector", "tensor", mm2)
            r.op("vector", lambda e, nb=nb: e.tensor_copy(
                out=x2l[0:1, nb * 512:(nb + 1) * 512], in_=p_a[0:1, :]))
            vh = r.op("vector", lambda e, nb=nb: e.tensor_copy(
                out=hrow[0:1, nb * 512:(nb + 1) * 512], in_=p_b[0:1, :]))
            r.wait("tensor", "vector", vh)
        x2l_done = vh
        r.wait("sync", "vector", vh)
        r.dma("sync", scr_dram.ap()[0:8, :].flatten().rearrange("(a p) -> a p", a=1), hrow[0:1, 0:1024])
        r.dwait("sync", "sync", r.dcnt["sync"])
        r.dma("sync", hcol[:], scr_dram.ap()[0:8, :].rearrange("a p -> p a"),
              allow_slow_non_contiguous=True)
        r.dwait("tensor", "sync", r.dcnt["sync"])
        for ec_ in range(8):
            mm = None
            for kc in range(8):
                mm = r.op("tensor", lambda e, ec_=ec_, kc=kc: e.matmul(
                    p_c[:, 0:1], wv_f[:, kc * 1024 + ec_ * 128: kc * 1024 + (ec_ + 1) * 128],
                    hcol[:, kc:kc + 1], start=(kc == 0), stop=(kc == 7)))
            r.wait("vector", "tensor", mm)
            ve = r.op("vector", lambda e, ec_=ec_: e.tensor_copy(out=vcol[:, ec_:ec_ + 1], in_=p_c[:, 0:1]))
            r.wait("tensor", "vector", ve)
        for ec_ in range(8):
            mm = None
            for kc in range(8):
                mm = r.op("tensor", lambda e, ec_=ec_, kc=kc: e.matmul(
                    p_c[:, 0:1], wfc_f[:, kc * 1024 + ec_ * 128: kc * 1024 + (ec_ + 1) * 128],
                    vcol[:, kc:kc + 1], start=(kc == 0), stop=(kc == 7)))
            r.wait("vector", "tensor", mm)
            ve = r.op("vector", lambda e, ec_=ec_: e.tensor_tensor(
                out=o2col[:, ec_:ec_ + 1], in0=p_c[:, 0:1], in1=bfc_c[:, ec_:ec_ + 1], op=ALU.add))
            r.wait("tensor", "vector", ve)
        vo = r.cnt["vector"]
        r.wait("sync", "vector", vo)
        r.dma("sync", scr_dram.ap()[8:16, :].rearrange("a p -> p a"), o2col[:],
              allow_slow_non_contiguous=True)
        r.dwait("sync", "sync", r.dcnt["sync"])
        dn = r.dma("sync", hrow[0:1, 3072:4096], scr_dram.ap()[8:16, :].flatten().rearrange("(a p) -> a p", a=1))
        r.dwait("tensor", "sync", dn)
        for nb in range(2):
            mm = r.op("tensor", lambda e, nb=nb: e.matmul(
                (p_a if nb == 0 else p_b)[:, :], ones_f[0:1, :],
                hrow[0:1, 3072 + nb * 512: 3072 + (nb + 1) * 512], start=True, stop=True))
            r.wait("vector", "tensor", mm)
            r.op("vector", lambda e, nb=nb: e.tensor_copy(
                out=o2rep[:, nb * 512:(nb + 1) * 512], in_=(p_a if nb == 0 else p_b)[:, :]))

        prev_store = 0
        for tt in range(17):
            if tt == 0:
                r.wait("vector", "scalar", r.cnt["scalar"])
                vz = r.op("vector", lambda e: e.memset(xt[:], 0.0))
                r.wait("sync", "vector", vz)
                dn = r.dma("sync", xt[127:128, :], x2l[0:1, :])
            else:
                dn = r.dma("sync", xt[:], x2o_d[(tt - 1) * 128: tt * 128, :])
            r.dwait("vector", "sync", dn)
            r.dwait("scalar", "sync", dn)
            emit_ln(r, xt[:], u_f[:], scr)
            r.op("vector", lambda e: e.tensor_tensor(out=h2_f[:], in0=u_f[:], in1=o2rep[:], op=ALU.add))
            emit_ln(r, h2_f[:], u_f[:], scr)
            r.op("vector", lambda e: e.tensor_tensor(out=u_f[:], in0=u_f[:], in1=gam_r[:], op=ALU.mult))
            r.op("vector", lambda e: e.tensor_tensor(out=u_f[:], in0=u_f[:], in1=bet_r[:], op=ALU.add))
            r.dwait("vector", "sync", prev_store)
            va2 = r.op("vector", lambda e: e.tensor_tensor(out=res[:], in0=xt[:], in1=u_f[:], op=ALU.add))
            r.wait("sync", "vector", va2)
            prev_store = r.dma("sync", o_writes(tt), res[:])
        r.dwait("sync", "sync", r.dcnt["sync"])




def emit_l1_ffn(nc, r, ctx, x3_d, wrT_d, wkT_d, wvT_d, mur_d, muk_d, o_d, pm_d=None):
    NT = 2176
    if True:
        sb_ = lambda name, shape, dt=F32: ctx.enter_context(nc.sbuf_tensor(_un(name), shape, dt))[:]
        ident, ones_bf, ones_f, _ = build_ident(r, sb_)
        wr_b = sb_("wr_b", [128, 8 * 1024], BF16)
        wk_b = sb_("wk_b", [128, 8 * 1024], BF16)
        wv_b = sb_("wv_b", [128, 8 * 1024], BF16)
        h_fm = sb_("h_fm", [128, 8 * NT], BF16)
        mixf = sb_("mixf", [128, 8 * TOK], BF16)
        r_bf = sb_("r_bf", [128, 8 * TOK], BF16)
        kact = sb_("kact", [128, 8 * TOK], BF16)
        outf = mixf
        mu_sb = sb_("mu_sb", [128, 8], F32)
        muk_sb = sb_("muk_sb", [128, 8], F32)
        om_sb = sb_("om_sb", [128, 8], F32)
        omk_sb = sb_("omk_sb", [128, 8], F32)
        h_bf = sb_("h_bf", [128, 1024], BF16)
        krel = sb_("krel", [128, 512], F32)
        xt = sb_("xt", [128, 1024], F32)
        res = sb_("res", [128, 1024], F32)
        stat = sb_("stat", [128, 24], F32)
        scr = dict(sq=sb_("sq", [128, 1024], F32), st=stat[:, 0:8])
        p_a = ctx.enter_context(nc.psum_tensor(_un("p_a"), [128, 512], F32))
        p_b = ctx.enter_context(nc.psum_tensor(_un("p_b"), [128, 512], F32))
        p_c2 = ctx.enter_context(nc.psum_tensor(_un("p_c2"), [128, 512], F32))
        p_d2 = ctx.enter_context(nc.psum_tensor(_un("p_d2"), [128, 512], F32))
        p_t0 = ctx.enter_context(nc.psum_tensor(_un("p_t0"), [128, 512], F32))
        p_t1 = ctx.enter_context(nc.psum_tensor(_un("p_t1"), [128, 512], F32))
        pp = PsumPool([p_t0, p_t1])
        ppsc = PsumPool([p_a, p_b, p_c2, p_d2])
        pm_sb = sb_("pm_sb", [128, 1], F32)

        for i in range(8):
            r.dma("gpsimd", wr_b[:, i * 1024:(i + 1) * 1024], wrT_d[i * 128:(i + 1) * 128, :])
            r.dma("gpsimd", wk_b[:, i * 1024:(i + 1) * 1024], wkT_d[i * 128:(i + 1) * 128, :])
            r.dma("gpsimd", wv_b[:, i * 1024:(i + 1) * 1024], wvT_d[i * 128:(i + 1) * 128, :])
        dm1 = r.dma("sync", mu_sb[:], mur_d.rearrange("(c p) -> p c", p=128), allow_slow_non_contiguous=True)
        dm2 = r.dma("sync", muk_sb[:], muk_d.rearrange("(c p) -> p c", p=128), allow_slow_non_contiguous=True)
        if pm_d is not None:
            dm2 = r.dma("sync", pm_sb[:], pm_d)
        r.dwait("vector", "sync", dm2)
        if pm_d is None:
            r.op("vector", lambda e: e.memset(pm_sb[:], 1.0))
        r.op("vector", lambda e: e.tensor_scalar(out=om_sb[:], in0=mu_sb[:], scalar1=-1.0, scalar2=1.0,
                                                 op0=ALU.mult, op1=ALU.add))
        r.op("vector", lambda e: e.tensor_scalar(out=omk_sb[:], in0=muk_sb[:], scalar1=-1.0, scalar2=1.0,
                                                 op0=ALU.mult, op1=ALU.add))

        xt2f = mixf[:, 0:4096].bitcast(F32)
        hb2f = mixf[:, 8192:10240]
        scr2f = [dict(sq=mixf[:, 4096:6144].bitcast(F32), st=stat[:, 0:8]),
                 dict(sq=mixf[:, 6144:8192].bitcast(F32), st=stat[:, 8:16])]
        lf_hist = []
        for tt in range(17):
            w = tt % 2
            gv, gt = lf_hist[tt - 2] if tt >= 2 else (0, 0)
            r.wait("sync", "vector", gv)
            dn = r.dma("sync", xt2f[:, w * 1024:(w + 1) * 1024], x3_d[tt * 128:(tt + 1) * 128, :])
            r.dwait("vector", "sync", dn)
            r.dwait("scalar", "sync", dn)
            r.wait("vector", "tensor", gt)
            vc = emit_ln(r, xt2f[:, w * 1024:(w + 1) * 1024], hb2f[:, w * 1024:(w + 1) * 1024],
                         scr2f[w], swait=gv)
            if tt == 0:
                vc = r.op("vector", lambda e: e.tensor_tensor(
                    out=hb2f[:, 0:1024], in0=hb2f[:, 0:1024],
                    in1=pm_sb[:].broadcast_to([128, 1024]), op=ALU.mult))
            r.wait("tensor", "vector", vc)
            pairs = [(hb2f[:, w * 1024 + j * 128: w * 1024 + (j + 1) * 128],
                      h_fm[:, j * NT + tt * 128: j * NT + (tt + 1) * 128]) for j in range(8)]
            transpose_tiles(r, ident, pairs, pp)
            lf_hist.append((vc, r.cnt["tensor"]))

        def mixes(mu, om, dst):
            r.wait("vector", "scalar", r.cnt["scalar"])
            for c in range(8):
                r.op("vector", lambda e, c=c, mu=mu: e.tensor_tensor(
                    out=dst[:, c * TOK:(c + 1) * TOK],
                    in0=h_fm[:, c * NT + 128: c * NT + 128 + TOK],
                    in1=mu[:, c:c + 1].broadcast_to([128, TOK]), op=ALU.mult))
                r.op("vector", lambda e, c=c, om=om: e.tensor_tensor(
                    out=kact[:, c * TOK:(c + 1) * TOK],
                    in0=h_fm[:, c * NT + 127: c * NT + 127 + TOK],
                    in1=om[:, c:c + 1].broadcast_to([128, TOK]), op=ALU.mult))
                r.op("vector", lambda e, c=c: e.tensor_tensor(
                    out=dst[:, c * TOK:(c + 1) * TOK], in0=dst[:, c * TOK:(c + 1) * TOK],
                    in1=kact[:, c * TOK:(c + 1) * TOK], op=ALU.add))
            return r.cnt["vector"]

        def proj(w_b, src, act):
            for ec_ in range(8):
                for nb in range(4):
                    t, slot = ppsc.acquire(r, "tensor")
                    mm = None
                    for kc in range(8):
                        mm = r.op("tensor", lambda e, ec_=ec_, nb=nb, kc=kc, t=t, w_b=w_b, src=src: e.matmul(
                            t[:, :], w_b[:, kc * 1024 + ec_ * 128: kc * 1024 + (ec_ + 1) * 128],
                            src[:, kc * TOK + nb * 512: kc * TOK + (nb + 1) * 512],
                            start=(kc == 0), stop=(kc == 7)))
                    cnt = act(ec_, nb, t, mm)
                    PsumPool.release(slot, "scalar" if cnt[0] == "s" else "vector", cnt[1])

        vmix = mixes(mu_sb, om_sb, mixf)
        r.wait("tensor", "vector", vmix)

        def act_sig(ec_, nb, t, mm):
            r.wait("scalar", "tensor", mm)
            c = r.op("scalar", lambda e, ec_=ec_, nb=nb, t=t: e.activation(
                out=r_bf[:, ec_ * TOK + nb * 512: ec_ * TOK + (nb + 1) * 512], in_=t[:, :], func=AF.Sigmoid))
            return ("s", c)
        proj(wr_b, mixf, act_sig)

        vmix = mixes(muk_sb, omk_sb, mixf)
        r.wait("tensor", "vector", vmix)

        def act_relu2(ec_, nb, t, mm):
            r.wait("scalar", "tensor", mm)
            c1 = r.op("scalar", lambda e, t=t: e.activation(out=krel[:, :], in_=t[:, :], func=AF.Relu))
            c2 = r.op("scalar", lambda e, ec_=ec_, nb=nb: e.activation(
                out=kact[:, ec_ * TOK + nb * 512: ec_ * TOK + (nb + 1) * 512], in_=krel[:, :], func=AF.Square))
            return ("s", c2)
        proj(wk_b, mixf, act_relu2)

        def act_vout(ec_, nb, t, mm):
            r.wait("vector", "tensor", mm)
            c = r.op("vector", lambda e, ec_=ec_, nb=nb, t=t: e.tensor_tensor(
                out=outf[:, ec_ * TOK + nb * 512: ec_ * TOK + (nb + 1) * 512], in0=t[:, :],
                in1=r_bf[:, ec_ * TOK + nb * 512: ec_ * TOK + (nb + 1) * 512], op=ALU.mult))
            return ("v", c)
        proj(wv_b, kact, act_vout)

        r.wait("tensor", "vector", r.cnt["vector"])
        prev_store = 0
        for tt in range(16):
            dn = r.dma("sync", xt[:], x3_d[128 + tt * 128: 128 + (tt + 1) * 128, :])
            pairs = [(outf[:, j * TOK + tt * 128: j * TOK + (tt + 1) * 128],
                      h_bf[:, j * 128:(j + 1) * 128]) for j in range(8)]
            transpose_tiles(r, ident, pairs, pp, eng_evac="vector")
            r.dwait("vector", "sync", dn)
            r.dwait("vector", "sync", prev_store)
            va = r.op("vector", lambda e: e.tensor_tensor(out=res[:], in0=xt[:], in1=h_bf[:], op=ALU.add))
            r.wait("sync", "vector", va)
            prev_store = r.dma("sync", o_d[tt * 128:(tt + 1) * 128, :], res[:])
        r.dwait("sync", "sync", r.dcnt["sync"])


def build_l1_ffn():
    nc = bass.Bass("TRN2", target_bir_lowering=False)
    x3_d = nc.dram_tensor("x3p", [2176, D], F32, kind="ExternalInput").ap()
    wrT_d = nc.dram_tensor("wrT", [D, D], F32, kind="ExternalInput").ap()
    wkT_d = nc.dram_tensor("wkT", [D, D], F32, kind="ExternalInput").ap()
    wvT_d = nc.dram_tensor("wvT", [D, D], F32, kind="ExternalInput").ap()
    mur_d = nc.dram_tensor("mur", [D], F32, kind="ExternalInput").ap()
    muk_d = nc.dram_tensor("muk", [D], F32, kind="ExternalInput").ap()
    o_d = nc.dram_tensor("o_x4", [TOK, D], F32, kind="ExternalOutput").ap()
    with ExitStack() as ctx:
        r = Rec(nc, ctx)
        with ExitStack() as s:
            emit_l1_ffn(nc, r, s, x3_d, wrT_d, wkT_d, wvT_d, mur_d, muk_d, o_d)
        blk = ctx.enter_context(nc.Block())
        r.finalize(blk)
    return nc


def barrier(r):
    snap = {e: r.cnt[e] for e in r.ENGS}
    dnap = {q: r.dcnt[q] for q in r.dsem}
    for e in r.ENGS:
        for p in r.ENGS:
            if p != e:
                r.wait(e, p, snap[p])
        for q in r.dsem:
            r.dwait(e, q, dnap[q])
        r.cwait(e, r.ccnt)


def build_full():
    nc = bass.Bass("TRN2", target_bir_lowering=False)
    xb_d = nc.dram_tensor("xb", [SEQ, D], F32, kind="ExternalInput").ap()
    xo_d = nc.dram_tensor("xo", [TOK, D], F32, kind="ExternalInput").ap()
    cm_d = nc.dram_tensor("cmask", [128, CMW], BF16, kind="ExternalInput").ap()
    wqT_d = nc.dram_tensor("wqT", [D, D], F32, kind="ExternalInput").ap()
    wkvd_d = nc.dram_tensor("wkvd", [D, 128], F32, kind="ExternalInput").ap()
    wkupT_d = nc.dram_tensor("wkupT", [128, 32], F32, kind="ExternalInput").ap()
    wvupT_d = nc.dram_tensor("wvupT", [128, 32], F32, kind="ExternalInput").ap()
    wo_d = nc.dram_tensor("wo", [D, D], F32, kind="ExternalInput").ap()
    wsl_d = nc.dram_tensor("wsl", [768, D], F32, kind="ExternalInput").ap()
    wdsl_d = nc.dram_tensor("wdsl", [128, 2816], F32, kind="ExternalInput").ap()
    wvT_d = nc.dram_tensor("wvT", [D, D], F32, kind="ExternalInput").ap()
    wfcT_d = nc.dram_tensor("wfcT", [D, D], F32, kind="ExternalInput").ap()
    bfc_d = nc.dram_tensor("bfc", [D], F32, kind="ExternalInput").ap()
    gam_d = nc.dram_tensor("gam", [D], F32, kind="ExternalInput").ap()
    bet_d = nc.dram_tensor("bet", [D], F32, kind="ExternalInput").ap()
    wrT_d = nc.dram_tensor("wrT", [D, D], F32, kind="ExternalInput").ap()
    wkT_d = nc.dram_tensor("wkT", [D, D], F32, kind="ExternalInput").ap()
    wvT2_d = nc.dram_tensor("wvT2", [D, D], F32, kind="ExternalInput").ap()
    mur_d = nc.dram_tensor("mur", [D], F32, kind="ExternalInput").ap()
    muk_d = nc.dram_tensor("muk", [D], F32, kind="ExternalInput").ap()
    pm_d = nc.dram_tensor("pm", [128, 1], F32, kind="ExternalInput").ap()
    sel_d = nc.dram_tensor("sel", [16, 2], F32, kind="ExternalInput").ap()
    o_d = nc.dram_tensor("o_x4", [TOK, D], F32, kind="ExternalOutput").ap()
    x1_dram = nc.dram_tensor("x1scr", [TOK, D], F32)
    x2_dram = nc.dram_tensor("x2scr", [TOK, D], F32)
    x3_dram = nc.dram_tensor("x3scr", [2176, D], F32)
    scr_dram = nc.dram_tensor("scrv", [16, 128], F32)
    ccin_dram = nc.dram_tensor("ccin", [2, D], F32)
    ccout_dram = nc.dram_tensor("ccout", [16, D], F32)
    wqin_dram = nc.dram_tensor("wqin", [128, 8960], BF16)
    wqout_dram = nc.dram_tensor("wqout", [1024, 8960], BF16)
    with ExitStack() as ctx:
        r = Rec(nc, ctx)
        with ExitStack() as s:
            emit_l0_attn(nc, r, s, xb_d, xo_d, cm_d, wqT_d, wkvd_d, wkupT_d, wvupT_d, wo_d,
                         x1_dram.ap())
        barrier(r)
        with ExitStack() as s:
            emit_l0_ffn(nc, r, s, x1_dram.ap(), wsl_d, wdsl_d, wqin_dram.ap(), wqout_dram.ap(),
                        x2_dram.ap())
        barrier(r)
        with ExitStack() as s:
            emit_l1_attn(nc, r, s, x2_dram.ap(), sel_d, ccin_dram.ap(), ccout_dram.ap(),
                         wvT_d, wfcT_d, bfc_d, gam_d, bet_d,
                         scr_dram, lambda tt: x3_dram.ap()[tt * 128:(tt + 1) * 128, :])
        barrier(r)
        with ExitStack() as s:
            emit_l1_ffn(nc, r, s, x3_dram.ap(), wrT_d, wkT_d, wvT2_d, mur_d, muk_d, o_d, pm_d=pm_d)
        blk = ctx.enter_context(nc.Block())
        r.finalize(blk)
    return nc


from concourse.bass_utils import run_bass_kernel_spmd

_NC_CACHE = {}


def _get(name, builder):
    if name not in _NC_CACHE:
        _NC_CACHE[name] = builder()
    return _NC_CACHE[name]


def kernel(**kw):
    import ml_dtypes
    f32 = np.float32
    g = lambda k: np.ascontiguousarray(np.asarray(kw[k], dtype=f32))
    x0 = g("L0_attn_x")
    cores = list(range(8))

    wq, wkd, wvd = g("L0_attn_wq"), g("L0_attn_wk_down"), g("L0_attn_wv_down")
    wku, wvu, wo = g("L0_attn_wk_up"), g("L0_attn_wv_up"), g("L0_attn_wo")
    wg_, wu_, wd_ = g("L0_ffn_w_gate"), g("L0_ffn_w_up"), g("L0_ffn_w_down")
    wqT = np.ascontiguousarray(wq.T)
    wkvd = np.ascontiguousarray(np.concatenate([wkd, wvd], axis=1))
    wkupT = np.ascontiguousarray(wku.T)
    wvupT = np.ascontiguousarray(wvu.T)
    wqkv, wfc = g("L1_attn_w_qkv"), g("L1_attn_w_fc")
    bfc, gam, bet = g("L1_attn_b_fc"), g("L1_attn_gamma"), g("L1_attn_beta")
    wvT = np.ascontiguousarray(wqkv[2048:3072].T)
    wfcT = np.ascontiguousarray(wfc.T)
    wr, wk_, wv_ = g("L1_ffn_W_r"), g("L1_ffn_W_k"), g("L1_ffn_W_v")
    mur, muk = g("L1_ffn_mu_r"), g("L1_ffn_mu_k")
    wrT = np.ascontiguousarray(wr.T)
    wkT = np.ascontiguousarray(wk_.T)
    wvT2 = np.ascontiguousarray(wv_.T)

    iu = np.arange(CMW)[None, :]
    pidx = np.arange(128)[:, None]
    w1024 = np.concatenate([wg_, wu_], axis=0).reshape(44, 128, 1024)
    in_maps = []
    for c in range(8):
        b, q0 = c // 4, (c % 4) * TOK
        cm = np.where(pidx <= (q0 + iu - 8064), 1.0, 0.0).astype(f32).astype(ml_dtypes.bfloat16)
        sel = np.zeros((16, 2), f32)
        if c % 4 != 0:
            sel[2 * (c - 1), 0] = 1.0
        for rr in range(4 * b, 4 * b + 4):
            sel[2 * rr + 1, 1] = 1.0
        pm = np.zeros((128, 1), f32) if c % 4 == 0 else np.ones((128, 1), f32)
        idxs = [(6 * c + s) if (6 * c + s) < 44 else 0 for s in range(6)]
        wsl = np.ascontiguousarray(w1024[idxs].reshape(768, 1024))
        wdsl = np.ascontiguousarray(wd_[c * 128:(c + 1) * 128, :])
        in_maps.append(dict(xb=x0[b], xo=x0[b, q0:q0 + TOK], cmask=cm,
                            wqT=wqT, wkvd=wkvd, wkupT=wkupT, wvupT=wvupT, wo=wo,
                            wsl=wsl, wdsl=wdsl,
                            wvT=wvT, wfcT=wfcT, bfc=bfc, gam=gam, bet=bet,
                            wrT=wrT, wkT=wkT, wvT2=wvT2, mur=mur, muk=muk,
                            pm=pm, sel=sel))
    res = run_bass_kernel_spmd(_get("FULL", build_full), in_maps, core_ids=cores)
    x4 = np.stack([np.concatenate([res.results[b * 4 + i]["o_x4"] for i in range(4)], 0)
                   for b in range(2)])
    return np.ascontiguousarray(x4.astype(np.float32))
